# version 11
# baseline (speedup 1.0000x reference)
import contextlib
import numpy as np

import concourse.bass as bass
import concourse.mybir as mybir
from concourse.bass_utils import run_bass_kernel_spmd

F32 = mybir.dt.float32
BF16 = mybir.dt.bfloat16
AF = mybir.ActivationFunctionType
ALU = mybir.AluOpType
AX = mybir.AxisListType

D = 2048
NC16 = 16
S = 2048
T = 1024
HD = 128
NHL = 8
M = 4
E = 8192
SCALE = float(HD) ** -0.5
EPS_MEM = 1e-6
EPS_LN = 1e-5
EPS_Q = 1e-8
WCNT = float(D * E)
PAIRS = [[0, 1], [2, 3], [4, 5], [6, 7]]
ALL8 = [list(range(8))]

ALL_STAGES = ("attn_a", "ffn1", "attn_b", "ffn2")
PARANOID = 0


def build(stages=ALL_STAGES):
    nc = bass.Bass("TRN2", target_bir_lowering=False, debug=False, num_devices=8)
    es = contextlib.ExitStack()
    sems = {}
    cnt = {}
    bank_free = {}

    def sem(nm):
        sems[nm] = es.enter_context(nc.semaphore(nm))
        cnt[nm] = 0

    def inc(bi, nm, k=1):
        bi.then_inc(sems[nm], k)
        cnt[nm] += k
        return cnt[nm]

    def incd(bi, nm):
        return inc(bi, nm, 16)

    def wt(eng, nm, val=None):
        eng.wait_ge(sems[nm], cnt[nm] if val is None else val)

    def pguard(bank):
        ent = bank_free.get(bank)
        if ent is not None:
            wt(t, ent[0], ent[1])

    def pdone(bank, nm):
        bank_free[bank] = (nm, cnt[nm])

    def wt_st(eng):
        for nm in ("st", "sb0", "sb1", "sf0", "sf1"):
            wt(eng, nm)

    def vfl(bi):
        n = inc(bi, "dve")
        nc.vector.wait_ge(sems["dve"], n)
        return n

    def sfl(bi):
        n = inc(bi, "act")
        nc.scalar.wait_ge(sems["act"], n)
        return n

    def DI(n, sh):
        return nc.dram_tensor(n, sh, F32, kind="ExternalInput").ap()

    xT = DI("xT", [D, T])
    outT = nc.dram_tensor("outT", [D, T], F32, kind="ExternalOutput").ap()
    wqTmy = DI("wqTmy", [D, T])
    wkTmy = DI("wkTmy", [D, T])
    wvTmy = DI("wvTmy", [D, T])
    woTmy = DI("woTmy", [T, D])
    memsI = DI("memsI", [M, NHL, HD, HD])
    mnT = DI("mnT", [NHL, HD, M])
    mnF = DI("mnF", [HD, M * 16])
    gateL = DI("gateL", [1, NHL])
    bWq = DI("bWq", [D, D])
    bWkmy = DI("bWkmy", [D, T])
    bWvmy = DI("bWvmy", [D, T])
    bWomy = DI("bWomy", [T, D])
    bbq = DI("bbq", [128, NC16])
    bbkv = DI("bbkv", [128, NHL * 2])
    bbo = DI("bbo", [128, NC16])
    wuS = DI("wuS", [256, E])
    wgS = DI("wgS", [256, E])
    wdS = DI("wdS", [1024, D])
    identC = DI("identC", [128, 128])
    sel4C = DI("sel4C", [4, 512])
    triuC = DI("triuC", [128, 128])
    bmaskC = DI("bmaskC", [128, 4 * 512])
    onesC = DI("onesC", [128, 2])
    onesR = DI("onesR", [1, 128])

    tern_u = nc.dram_tensor("tern_u", [256, E], BF16).ap()
    tern_g = nc.dram_tensor("tern_g", [256, E], BF16).ap()
    tern_d = nc.dram_tensor("tern_d", [1024, D], BF16).ap()
    wu_q = nc.dram_tensor("wu_q", [D, E], BF16).ap()
    wg_q = nc.dram_tensor("wg_q", [D, E], BF16).ap()
    wd_q = nc.dram_tensor("wd_q", [E, D], BF16).ap()
    sc_in = nc.dram_tensor("sc_in", [1, 4], F32).ap()
    sc_out = nc.dram_tensor("sc_out", [1, 4], F32).ap()
    xag_in = nc.dram_tensor("xag_in", [D, T], BF16).ap()
    xag_out = nc.dram_tensor("xag_out", [2, D, T], BF16).ap()
    aqkvL = nc.dram_tensor("aqkvL", [3, T, S], BF16).ap()
    amixL = nc.dram_tensor("amixL", [T, S], BF16).ap()
    ops_in = nc.dram_tensor("ops_in", [2, D, T], F32).ap()
    ops_out = nc.dram_tensor("ops_out", [D, T], F32).ap()
    bkvL = nc.dram_tensor("bkvL", [2, T, S], BF16).ap()
    bmixL = nc.dram_tensor("bmixL", [T, S], BF16).ap()

    R = es.enter_context(nc.sbuf_tensor("R", [128, NC16 * T], F32))
    AR = es.enter_context(nc.sbuf_tensor("AR", [128, 28672], F32))
    XL = AR[:, 0:8192].bitcast(BF16)
    WS = AR[:, 8192:12288]
    BIG = AR[:, 12288:28672]

    IDT = es.enter_context(nc.sbuf_tensor("IDT", [128, 128], BF16))
    IDF = es.enter_context(nc.sbuf_tensor("IDF", [128, 4], F32))
    TRIU = es.enter_context(nc.sbuf_tensor("TRIU", [128, 128], BF16))
    BMASK = es.enter_context(nc.sbuf_tensor("BMASK", [128, 4 * 512], BF16))
    ONESB = es.enter_context(nc.sbuf_tensor("ONESB", [128, 2], BF16))
    ONESF = es.enter_context(nc.sbuf_tensor("ONESF", [128, 2], F32))
    ONESRF = es.enter_context(nc.sbuf_tensor("ONESRF", [1, 128], F32))
    BIAS = es.enter_context(nc.sbuf_tensor("BIAS", [128, 48], F32))
    MNT = es.enter_context(nc.sbuf_tensor("MNT", [128, NHL * M], BF16))
    MNF = es.enter_context(nc.sbuf_tensor("MNF", [128, 64], F32))
    GROW = es.enter_context(nc.sbuf_tensor("GROW", [1, 64], F32))
    SCOL = es.enter_context(nc.sbuf_tensor("SCOL", [128, 12], F32))
    P0 = es.enter_context(nc.sbuf_tensor("P0", [1, 2048], F32))
    SEL4 = es.enter_context(nc.sbuf_tensor("SEL4", [4, 512], F32))
    SROW = es.enter_context(nc.sbuf_tensor("SROW", [1, 64], F32))
    GCOL = es.enter_context(nc.sbuf_tensor("GCOL", [128, 8], F32))
    SST = es.enter_context(nc.sbuf_tensor("SST", [128, 128], F32))
    SSTB = es.enter_context(nc.sbuf_tensor("SSTB", [128, 128], BF16))
    ZST = es.enter_context(nc.sbuf_tensor("ZST", [128, 4], F32))
    ZSTB = es.enter_context(nc.sbuf_tensor("ZSTB", [128, 2], BF16))
    MEMW = es.enter_context(nc.sbuf_tensor("MEMW", [128, M * HD], BF16))
    TMP = es.enter_context(nc.sbuf_tensor("TMP", [128, 512], F32))
    TMP2 = es.enter_context(nc.sbuf_tensor("TMP2", [128, 512], F32))
    OST = es.enter_context(nc.sbuf_tensor("OST", [128, 1024], F32))
    QSQ = es.enter_context(nc.sbuf_tensor("QSQ", [128, 1024], BF16))
    SP = es.enter_context(nc.sbuf_tensor("SP", [128, 48], F32))

    pA = es.enter_context(nc.psum_tensor("pA", [128, 512], F32))
    pB = es.enter_context(nc.psum_tensor("pB", [128, 512], F32))
    pC = es.enter_context(nc.psum_tensor("pC", [128, 512], F32))
    pD = es.enter_context(nc.psum_tensor("pD", [128, 512], F32))
    pE = es.enter_context(nc.psum_tensor("pE", [128, 512], F32))
    pF = es.enter_context(nc.psum_tensor("pF", [128, 512], F32))
    pS1 = es.enter_context(nc.psum_tensor("pS1", [128, 512], F32))
    pS2 = es.enter_context(nc.psum_tensor("pS2", [128, 512], F32))
    PN = {id(pA): "pA", id(pB): "pB", id(pC): "pC", id(pD): "pD",
          id(pE): "pE", id(pF): "pF", id(pS1): "pS1", id(pS2): "pS2"}

    for nm in ("ld", "ldw", "gw", "gld", "st", "mm", "dve", "act", "cc",
               "q0", "q1", "w0", "w1", "g0", "g1", "sb0", "sb1", "sf0", "sf1",
               "bt", "bv", "bs", "bq", "bg"):
        sem(nm)

    t = nc.tensor
    v = nc.vector
    sc = nc.scalar
    q = nc.sync
    g = nc.gpsimd

    def barrier():
        for e, nm in ((t, "bt"), (v, "bv"), (sc, "bs"), (q, "bq"), (g, "bg")):
            inc(e.nop(), nm)
        for e in (t, v, sc, q, g):
            for nm in ("bt", "bv", "bs", "bq", "bg"):
                wt(e, nm)

    incd(g.dma_start(out=IDT[:, :], in_=identC[:, :]), "gld")
    incd(q.dma_start(out=IDF[:, 0:4], in_=identC[:, 0:4]), "ld")
    incd(q.dma_start(out=SEL4[:, :], in_=sel4C[:, :]), "ld")
    incd(g.dma_start(out=TRIU[:, :], in_=triuC[:, :]), "gld")
    incd(g.dma_start(out=BMASK[:, :], in_=bmaskC[:, :]), "gld")
    incd(g.dma_start(out=ONESB[:, :], in_=onesC[:, :]), "gld")
    incd(q.dma_start(out=ONESF[:, :], in_=onesC[:, :]), "ld")
    incd(q.dma_start(out=ONESRF[:, :], in_=onesR[:, :]), "ld")
    incd(g.dma_start(out=MNT.rearrange("p (h m) -> p h m", h=NHL),
                     in_=mnT.transpose([1, 0, 2])), "gld")
    incd(q.dma_start(out=MNF[:, :], in_=mnF[:, :]), "ld")
    incd(q.dma_start(out=GROW[0:1, 0:NHL], in_=gateL[0:1, :]), "ld")
    incd(q.dma_start(out=BIAS[:, 0:16], in_=bbq[:, :]), "ld")
    incd(q.dma_start(out=BIAS[:, 16:32], in_=bbkv[:, :]), "ld")
    incd(q.dma_start(out=BIAS[:, 32:48], in_=bbo[:, :]), "ld")
    incd(q.dma_start(out=R.rearrange("p (c t) -> p c t", c=NC16),
                     in_=xT.rearrange("(c p) t -> p c t", p=128)), "ld")
    wt(sc, "ld")
    wt(sc, "gld")
    inc(sc.activation(out=GROW[0:1, 16:24], in_=GROW[0:1, 0:8], func=AF.Sigmoid), "act")
    wt(v, "act")
    inc(v.tensor_scalar(out=GROW[0:1, 8:16], in0=GROW[0:1, 16:24], scalar1=-1.0, scalar2=1.0,
                        op0=ALU.mult, op1=ALU.add), "dve")
    inc(v.tensor_reduce(out=GCOL[:, 0:4], in_=MNF.rearrange("p (m h) -> p m h", m=M),
                        axis=AX.X, op=ALU.add), "dve")
    wt(t, "dve")
    pguard("pS1")
    inc(t.matmul(pS1[0:1, 0:4], ONESF[:, 0:1], GCOL[:, 0:4], start=True, stop=True), "mm")
    wt(v, "mm")
    inc(v.tensor_single_scalar(out=SROW[0:1, 60:64], in_=pS1[0:1, 0:4], scalar=EPS_MEM,
                               op=ALU.is_ge), "dve")
    pdone("pS1", "dve")

    WSL = ((wuS, 2, E, tern_u, 0), (wgS, 2, E, tern_g, 1), (wdS, 8, D, tern_d, 2))
    WSb = (WS[:, 0:1024], WS[:, 1024:2048])
    dve_n = {}
    ia = 0
    for (wsl, nch, width, _td, i) in WSL:
        for it in range(nch * (width // 1024)):
            buf = WSb[ia % 2]
            ch, off = divmod(it, width // 1024)
            if ia >= 2:
                wt(q, "dve", dve_n[ia - 2])
            qs = "q%d" % (ia % 2)
            incd(q.dma_start(out=buf[:, :], in_=wsl[ch * 128:(ch + 1) * 128,
                                                    off * 1024:(off + 1) * 1024]), qs)
            wt(v, qs)
            dve_n[ia] = inc(v.tensor_reduce(out=SP[:, ia:ia + 1], in_=buf[:, :],
                                            axis=AX.X, op=ALU.add,
                                            apply_absolute_value=True), "dve")
            ia += 1
    wt(v, "dve")
    for i in range(3):
        inc(v.tensor_reduce(out=GCOL[:, 4 + i:5 + i], in_=SP[:, 16 * i:16 * i + 16],
                            axis=AX.X, op=ALU.add), "dve")
    wt(t, "dve")
    pguard("pS2")
    inc(t.matmul(pS2[0:1, 0:3], ONESF[:, 0:1], GCOL[:, 4:7], start=True, stop=True), "mm")
    wt(v, "mm")
    inc(v.tensor_copy(out=SROW[0:1, 0:3], in_=pS2[0:1, 0:3]), "dve")
    pdone("pS2", "dve")
    inc(v.memset(SROW[0:1, 3:4], 0.0), "dve")
    wt(q, "dve")
    incd(q.dma_start(out=sc_in[0:1, 0:4], in_=SROW[0:1, 0:4]), "st")
    wt(g, "st")
    inc(g.collective_compute("AllReduce", ALU.add, replica_groups=ALL8,
                             ins=[sc_in[:, :]], outs=[sc_out[:, :]]), "cc")
    wt(q, "cc")
    incd(q.dma_start(out=SROW[0:1, 4:8], in_=sc_out[0:1, 0:4]), "ld")
    wt(v, "ld")
    vfl(v.tensor_scalar(out=SROW[0:1, 8:11], in0=SROW[0:1, 4:7], scalar1=1.0 / WCNT,
                        scalar2=EPS_Q, op0=ALU.mult, op1=ALU.add))
    vfl(v.tensor_scalar_mul(out=SROW[0:1, 12:15], in0=SROW[0:1, 8:11], scalar1=0.5))
    vfl(v.tensor_scalar_mul(out=SROW[0:1, 16:19], in0=SROW[0:1, 12:15], scalar1=-1.0))
    wt(t, "dve")
    pguard("pS2")
    inc(t.matmul(pS2[:, 0:3], ONESRF[0:1, :], SROW[0:1, 12:15], start=True, stop=True), "mm")
    inc(t.matmul(pS2[:, 4:7], ONESRF[0:1, :], SROW[0:1, 16:19], start=True, stop=True), "mm")
    wt(v, "mm")
    inc(v.tensor_copy(out=SCOL[:, 0:3], in_=pS2[:, 0:3]), "dve")
    vfl(v.tensor_copy(out=SCOL[:, 4:7], in_=pS2[:, 4:7]))
    pdone("pS2", "dve")
    inc(v.tensor_scalar_mul(out=SCOL[:, 8:11], in0=SCOL[:, 4:7], scalar1=-1.0), "dve")
    s1b = BIG[:, 0:512].bitcast(BF16)
    s2b = BIG[:, 512:1024].bitcast(BF16)
    sob = BIG[:, 1024:1536].bitcast(BF16)
    st_n = {}
    ia = 0
    wt(sc, "dve")
    for (wsl, nch, width, td, i) in WSL:
        for it in range(nch * (width // 1024)):
            buf = WSb[ia % 2]
            ch, off = divmod(it, width // 1024)
            if ia >= 2:
                wt(q, "act", st_n[ia - 2][1])
            qs = "q%d" % (ia % 2)
            incd(q.dma_start(out=buf[:, :], in_=wsl[ch * 128:(ch + 1) * 128,
                                                    off * 1024:(off + 1) * 1024]), qs)
            wt(sc, qs)
            if ia >= 1:
                wt(sc, "st", st_n[ia - 1][0])
            inc(sc.activation(out=s1b[:, :], in_=buf[:, :], func=AF.Sign, bias=SCOL[:, 4 + i:5 + i]), "act")
            a2 = inc(sc.activation(out=s2b[:, :], in_=buf[:, :], func=AF.Sign, bias=SCOL[:, 8 + i:9 + i]), "act")
            wt(v, "act")
            inc(v.tensor_add(out=sob[:, :], in0=s1b[:, :], in1=s2b[:, :]), "dve")
            wt(q, "dve")
            st_n[ia] = (incd(q.dma_start(out=td[ch * 128:(ch + 1) * 128, off * 1024:(off + 1) * 1024],
                                         in_=sob[:, :]), "st"), a2)
            ia += 1
    wt(g, "st")
    inc(g.collective_compute("AllGather", ALU.bypass, replica_groups=ALL8,
                             ins=[tern_u[:, :]], outs=[wu_q[:, :]]), "cc")
    inc(g.collective_compute("AllGather", ALU.bypass, replica_groups=ALL8,
                             ins=[tern_g[:, :]], outs=[wg_q[:, :]]), "cc")
    inc(g.collective_compute("AllGather", ALU.bypass, replica_groups=ALL8,
                             ins=[tern_d[:, :]], outs=[wd_q[:, :]]), "cc")
    cc_tern = cnt["cc"]
    barrier()

    def emit_ln():
        for th in range(2):
            to = th * 512
            pguard("pS1")
            pguard("pS2")
            for dc in range(NC16):
                rsl = R[:, dc * T + to: dc * T + to + 512]
                xsl = XL[:, dc * T + to: dc * T + to + 512]
                sq = QSQ[:, (dc % 2) * 512:(dc % 2) * 512 + 512]
                if dc >= 2:
                    wt(v, "mm")
                vfl(v.tensor_copy(out=xsl, in_=rsl))
                inc(v.tensor_mul(out=sq, in0=xsl, in1=xsl), "dve")
                wt(t, "dve")
                inc(t.matmul(pS1[0:1, :], ONESB[:, 0:1], xsl, start=(dc == 0), stop=(dc == NC16 - 1)), "mm")
                inc(t.matmul(pS2[0:1, :], ONESB[:, 0:1], sq, start=(dc == 0), stop=(dc == NC16 - 1)), "mm")
            wt(v, "mm")
            mean = P0[0:1, 0:512]
            var = P0[0:1, 512:1024]
            inv = P0[0:1, 1024:1536]
            nr = P0[0:1, 1536:2048]
            vfl(v.tensor_scalar_mul(out=mean, in0=pS1[0:1, :], scalar1=1.0 / D))
            vfl(v.tensor_scalar_mul(out=nr, in0=pS2[0:1, :], scalar1=1.0 / D))
            pdone("pS1", "dve")
            pdone("pS2", "dve")
            vfl(v.tensor_mul(out=var, in0=mean, in1=mean))
            vfl(v.scalar_tensor_tensor(out=var, in0=var, scalar=-1.0, in1=nr,
                                       op0=ALU.mult, op1=ALU.add))
            vfl(v.tensor_scalar_add(out=var, in0=var, scalar1=EPS_LN))
            inc(v.reciprocal(out=inv, in_=var), "dve")
            wt(sc, "dve")
            inc(sc.activation(out=inv, in_=inv, func=AF.Sqrt), "act")
            wt(v, "act")
            vfl(v.tensor_mul(out=nr, in0=inv, in1=inv))
            vfl(v.tensor_mul(out=nr, in0=nr, in1=var))
            vfl(v.tensor_scalar(out=nr, in0=nr, scalar1=-0.5, scalar2=1.5, op0=ALU.mult, op1=ALU.add))
            inc(v.tensor_mul(out=inv, in0=inv, in1=nr), "dve")
            wt(t, "dve")
            pguard("pE")
            pguard("pF")
            inc(t.matmul(pE[:, :], ONESRF[0:1, :], mean, start=True, stop=True), "mm")
            inc(t.matmul(pF[:, :], ONESRF[0:1, :], inv, start=True, stop=True), "mm")
            wt(v, "mm")
            for dc in range(NC16):
                rsl = R[:, dc * T + to: dc * T + to + 512]
                xsl = XL[:, dc * T + to: dc * T + to + 512]
                vfl(v.tensor_sub(out=TMP[:, :], in0=rsl, in1=pE[:, :]))
                vfl(v.tensor_mul(out=xsl, in0=TMP[:, :], in1=pF[:, :]))
            pdone("pE", "dve")
            pdone("pF", "dve")

    def cached_w_load(w_dram, n_dc, dst_bf):
        cols = w_dram.shape[1]
        n = None
        for dd in range(0, n_dc, 4):
            nn = min(4, n_dc - dd)
            src = w_dram.rearrange("(dc p) e -> p dc e", p=128)[:, dd:dd + nn, :]
            dstv = dst_bf.rearrange("p (dc e) -> p dc e", dc=n_dc)[:, dd:dd + nn, :]
            n = incd(g.dma_start(out=dstv, in_=src), "gw")
        return n

    def mm_cached(wbuf, n_dc, n_ec, rhs_fn, sink):
        cols = n_ec * 128
        for ec in range(n_ec):
            ps = (pA, pB)[ec % 2]
            pguard(PN[id(ps)])
            for dc in range(n_dc):
                inc(t.matmul(ps[:, :], wbuf[:, dc * cols + ec * 128: dc * cols + ec * 128 + 128],
                             rhs_fn(dc), start=(dc == 0), stop=(dc == n_dc - 1)), "mm")
            sink(ec, ps)

    def load_xag_rhs(tt):
        blk, half = divmod(tt, 2)
        n = None
        for dd in range(0, NC16, 4):
            src = xag_out[blk].rearrange("(dc p) t -> p dc t", p=128)[:, dd:dd + 4,
                                                                     half * 512:half * 512 + 512]
            dstv = XL.rearrange("p (dc t) -> p dc t", dc=NC16)[:, dd:dd + 4, 0:512]
            n = incd(q.dma_start(out=dstv, in_=src), "ld")
        return n

    st_prev = {}

    def store_sink(dst_fn, bias_col=None):
        def sink(ec, ps):
            wt(sc, "mm")
            half = ec % 2
            stg = QSQ[:, half * 512:half * 512 + 512]
            if st_prev.get(("b", half)):
                wt(sc, "sb%d" % half, st_prev[("b", half)])
            if bias_col is not None:
                bi = sc.activation(out=stg, in_=ps[:, :], func=AF.Identity, bias=bias_col(ec))
            else:
                bi = sc.activation(out=stg, in_=ps[:, :], func=AF.Copy)
            n = inc(bi, "act")
            pdone(PN[id(ps)], "act")
            wt(q, "act", n)
            st_prev[("b", half)] = incd(q.dma_start(out=dst_fn(ec), in_=stg), "sb%d" % half)
        return sink

    def store_sink_f32(dst_fn):
        def sink(ec, ps):
            wt(v, "mm")
            half = ec % 2
            stg = OST[:, half * 512:half * 512 + 512]
            if st_prev.get(("f", half)):
                wt(v, "sf%d" % half, st_prev[("f", half)])
            n = inc(v.tensor_copy(out=stg, in_=ps[:, :]), "dve")
            pdone(PN[id(ps)], "dve")
            wt(q, "dve", n)
            st_prev[("f", half)] = incd(q.dma_start(out=dst_fn(ec), in_=stg), "sf%d" % half)
        return sink

    def ag_xln():
        wt(q, "dve")
        incd(q.dma_start(out=xag_in.rearrange("(dc p) t -> p dc t", p=128),
                         in_=XL.rearrange("p (dc t) -> p dc t", dc=NC16)), "st")
        wt_st(g)
        inc(g.collective_compute("AllGather", ALU.bypass, replica_groups=PAIRS,
                                 ins=[xag_in[:, :]], outs=[xag_out[:, :, :]]), "cc")
        wt(q, "cc")

    def proj_heads(w_list, dstL, wcache, bias_of=None):
        for wi, wd in enumerate(w_list):
            cached_w_load(wd, NC16, wcache)
            for tt in range(4):
                load_xag_rhs(tt)
                wt(t, "ld")
                wt(t, "gw")
                bias_col = None
                if bias_of is not None:
                    bias_col = lambda ec, _wi=wi: bias_of(_wi, ec)
                sink = store_sink(lambda ec, _wi=wi, _tt=tt: dstL[_wi, ec * 128:(ec + 1) * 128,
                                                                 _tt * 512:(_tt + 1) * 512],
                                  bias_col=bias_col)
                mm_cached(wcache, NC16, NHL, lambda dc: XL[:, dc * T: dc * T + 512], sink)
                barrier()

    def oproj_partial(w_my, mixL, bias_idx=None):
        wcache = BIG[:, 0:8192].bitcast(BF16)
        cached_w_load(w_my, NHL, wcache)
        wt_st(q)
        for tt in range(4):
            for dd in range(NHL):
                incd(q.dma_start(out=XL.rearrange("p (dc t) -> p dc t", dc=NC16)[:, dd, 0:512],
                                 in_=mixL[dd * 128:(dd + 1) * 128, tt * 512:(tt + 1) * 512]), "ld")
            wt(t, "ld")
            wt(t, "gw")
            blk, half = divmod(tt, 2)
            sink = store_sink_f32(lambda ec, _b=blk, _h=half: ops_in[_b, ec * 128:(ec + 1) * 128,
                                                                    _h * 512:(_h + 1) * 512])
            mm_cached(wcache, NHL, NC16, lambda dc: XL[:, dc * T: dc * T + 512], sink)
            barrier()
        wt_st(g)
        inc(g.collective_compute("ReduceScatter", ALU.add, replica_groups=PAIRS,
                                 ins=[ops_in[:, :, :]], outs=[ops_out[:, :]]), "cc")
        wt(q, "cc")
        for dc in range(NC16):
            wt(q, "dve")
            incd(q.dma_start(out=TMP[:, :], in_=ops_out[dc * 128:(dc + 1) * 128, 0:512]), "ld")
            incd(q.dma_start(out=TMP2[:, :], in_=ops_out[dc * 128:(dc + 1) * 128, 512:1024]), "ld")
            wt(v, "ld")
            for th, tb in ((0, TMP), (1, TMP2)):
                rsl = R[:, dc * T + th * 512: dc * T + th * 512 + 512]
                if bias_idx is not None:
                    bi = v.scalar_tensor_tensor(out=rsl, in0=tb[:, :],
                                                scalar=BIAS[:, bias_idx + dc:bias_idx + dc + 1],
                                                in1=rsl, op0=ALU.add, op1=ALU.add)
                else:
                    bi = v.tensor_add(out=rsl, in0=rsl, in1=tb[:, :])
                inc(bi, "dve")

    if "attn_a" in stages:
        emit_ln()
        ag_xln()
        barrier()
        proj_heads((wqTmy, wkTmy, wvTmy), aqkvL, BIG[:, 0:8192].bitcast(BF16))
        wt_st(q)
        qF = BIG[:, 0:1024].bitcast(BF16)
        qP = BIG[:, 1024:2048].bitcast(BF16)
        kP = BIG[:, 2048:3072].bitcast(BF16)
        kPT = BIG[:, 3072:4096].bitcast(BF16)
        vT = BIG[:, 4096:5120].bitcast(BF16)
        vF = BIG[:, 5120:6144].bitcast(BF16)
        maccb = BIG[:, 6144:7168].bitcast(BF16)
        lout = BIG[:, 7168:8192].bitcast(BF16)
        nrow = BIG[0:4, 8192:10240]
        crow = BIG[0:4, 10240:12288]
        oout = BIG[:, 12288:13312].bitcast(BF16)
        tsc1 = BIG[:, 13312:14336].bitcast(BF16)
        rdsb = BIG[:, 14336:14464]
        qpm = BIG[:, 14464:14720].bitcast(BF16)
        for gl in range(NHL):
            wt(q, "dve")
            wt_st(q)
            for wi, dstt in ((0, qF), (1, kP), (2, vF)):
                incd(q.dma_start(out=dstt[:, :], in_=aqkvL[wi, gl * 128:(gl + 1) * 128, :]), "ld")
            wt(g, "mm")
            incd(g.dma_start(out=MEMW.rearrange("p (m e) -> p m e", m=M),
                             in_=memsI[:, gl, :, :].transpose([1, 0, 2])), "gw")
            wt(sc, "ld")
            wt(v, "ld")
            for (src, dst) in ((qF, qP), (kP, kP)):
                wt(sc, "dve")
                inc(sc.activation(out=tsc1[:, :], in_=src[:, :], func=AF.Exp), "act")
                wt(v, "act")
                vfl(v.tensor_scalar_min(out=tsc1[:, :], in0=tsc1[:, :], scalar1=1.0))
                inc(v.scalar_tensor_tensor(out=dst[:, :], in0=src[:, :], scalar=0.0,
                                           in1=tsc1[:, :], op0=ALU.max, op1=ALU.add), "dve")
            wt(t, "dve")
            for (srcT, dstT) in ((kP, kPT), (vF, vT)):
                for ch in range(16):
                    ps = (pC, pD)[ch % 2]
                    psb = ps[:, 0:64].bitcast(BF16)
                    pguard(PN[id(ps)])
                    inc(t.transpose(psb, srcT[:, ch * 128:(ch + 1) * 128], IDT[:, :]), "mm")
                    wt(v, "mm")
                    inc(v.tensor_copy(out=dstT[:, ch * 128:(ch + 1) * 128], in_=psb), "dve")
                    pdone(PN[id(ps)], "dve")
            wt(t, "gw")
            for st4 in range(4):
                pguard("pS1")
                wt(t, "dve")
                inc(t.matmul(pS1[0:4, :], MNT[:, gl * M:(gl + 1) * M], qP[:, st4 * 512:(st4 + 1) * 512],
                             start=True, stop=True), "mm")
                wt(v, "mm")
                inc(v.tensor_copy(out=nrow[:, st4 * 512:(st4 + 1) * 512], in_=pS1[0:4, :]), "dve")
                pdone("pS1", "dve")
            vfl(v.tensor_reduce(out=GCOL[:, 4:5], in_=qF[:, :], axis=AX.X, op=ALU.add))
            inc(v.tensor_copy(out=QSQ[:, 0:1], in_=GCOL[:, 4:5]), "dve")
            wt(t, "dve")
            pguard("pS2")
            inc(t.matmul(pS2[0:1, 0:4], QSQ[:, 0:1], MNT[:, gl * M:(gl + 1) * M],
                         start=True, stop=True), "mm")
            wt(v, "mm")
            relm = SROW[0:1, 40:44]
            vfl(v.tensor_scalar_mul(out=relm, in0=pS2[0:1, 0:4], scalar1=SCALE / S))
            pdone("pS2", "dve")
            vfl(v.tensor_scalar(out=SROW[0:1, 44:48], in0=SROW[0:1, 60:64], scalar1=1e30,
                                scalar2=-1e30, op0=ALU.mult, op1=ALU.add))
            vfl(v.tensor_add(out=relm, in0=relm, in1=SROW[0:1, 44:48]))
            vfl(v.tensor_reduce(out=SROW[0:1, 48:49], in_=relm, axis=AX.X, op=ALU.max))
            vfl(v.scalar_tensor_tensor(out=relm, in0=relm, scalar=0.0,
                                       in1=SROW[0:1, 48:49].to_broadcast([1, 4]),
                                       op0=ALU.add, op1=ALU.subtract))
            wt(sc, "dve")
            inc(sc.activation(out=relm, in_=relm, func=AF.Exp), "act")
            wt(v, "act")
            vfl(v.tensor_reduce(out=SROW[0:1, 48:49], in_=relm, axis=AX.X, op=ALU.add))
            vfl(v.reciprocal(out=SROW[0:1, 49:50], in_=SROW[0:1, 48:49]))
            vfl(v.scalar_tensor_tensor(out=relm, in0=relm, scalar=0.0,
                                       in1=SROW[0:1, 49:50].to_broadcast([1, 4]),
                                       op0=ALU.add, op1=ALU.mult))
            vfl(v.tensor_mul(out=relm, in0=relm, in1=SROW[0:1, 60:64]))
            vfl(v.scalar_tensor_tensor(out=relm, in0=relm, scalar=0.0,
                                       in1=GROW[0:1, 16 + gl:17 + gl].to_broadcast([1, 4]),
                                       op0=ALU.add, op1=ALU.mult))
            wt(t, "dve")
            pguard("pS2")
            inc(t.matmul(pS2[0:4, 0:1], relm, ONESF[0:1, 0:1], start=True, stop=True), "mm")
            wt(v, "mm")
            vfl(v.tensor_copy(out=GCOL[0:4, 6:7], in_=pS2[0:4, 0:1]))
            pdone("pS2", "dve")
            vfl(v.tensor_scalar_max(out=nrow[:, :], in0=nrow[:, :], scalar1=EPS_MEM))
            vfl(v.reciprocal(out=crow[:, :], in_=nrow[:, :]))
            inc(v.tensor_scalar_mul(out=crow[:, :], in0=crow[:, :], scalar1=GCOL[0:4, 6:7]), "dve")
            for st4 in range(4):
                pguard("pB")
                for m4 in range(M):
                    pguard("pE")
                    wt(t, "dve")
                    inc(t.matmul(pE[:, :], SEL4[0:4, m4 * 128:(m4 + 1) * 128],
                                 crow[0:4, st4 * 512:(st4 + 1) * 512],
                                 start=True, stop=True), "mm")
                    wt(v, "mm")
                    inc(v.tensor_mul(out=qpm[:, :], in0=qP[:, st4 * 512:(st4 + 1) * 512], in1=pE[:, :]), "dve")
                    pdone("pE", "dve")
                    wt(t, "dve")
                    inc(t.matmul(pB[:, :], MEMW[:, m4 * HD:(m4 + 1) * HD], qpm[:, :],
                                 start=(m4 == 0), stop=(m4 == M - 1)), "mm")
                wt(v, "mm")
                inc(v.tensor_copy(out=maccb[:, st4 * 512:(st4 + 1) * 512], in_=pB[:, :]), "dve")
                pdone("pB", "dve")
            inc(v.memset(SST[:, :], 0.0), "dve")
            inc(v.memset(SSTB[:, :], 0.0), "dve")
            inc(v.memset(ZST[:, 0:1], 0.0), "dve")
            vfl(v.memset(ZSTB[:, 0:1], 0.0))
            for ch in range(16):
                o = ch * 128
                wt(t, "dve")
                pguard("pC")
                pguard("pD")
                pguard("pA")
                pguard("pS1")
                inc(t.matmul(pC[:, 0:128], kP[:, o:o + 128], qP[:, o:o + 128], start=True, stop=True), "mm")
                inc(t.matmul(pD[:, 0:128], kPT[:, o:o + 128], vT[:, o:o + 128], start=True, stop=True), "mm")
                inc(t.matmul(pA[:, 0:128], SSTB[:, :], qP[:, o:o + 128], start=True, stop=False), "mm")
                inc(t.matmul(pS1[0:1, 0:128], ZSTB[:, 0:1], qP[:, o:o + 128], start=True, stop=False), "mm")
                wt(v, "mm")
                mskd = tsc1[:, 0:128]
                inc(v.tensor_mul(out=mskd, in0=pC[:, 0:128], in1=TRIU[:, :]), "dve")
                pdone("pC", "dve")
                wt(t, "dve")
                inc(t.matmul(pA[:, 0:128], vT[:, o:o + 128], mskd, start=False, stop=True), "mm")
                inc(t.matmul(pS1[0:1, 0:128], ONESB[:, 0:1], mskd, start=False, stop=True), "mm")
                wt(v, "mm")
                den = P0[0:1, 0:128]
                vfl(v.tensor_single_scalar(out=den, in_=pS1[0:1, 0:128], scalar=EPS_MEM, op=ALU.max))
                inc(v.reciprocal(out=den, in_=den), "dve")
                pdone("pS1", "dve")
                wt(t, "dve")
                pguard("pF")
                inc(t.matmul(pF[:, 0:128], ONESRF[0:1, :], den, start=True, stop=True), "mm")
                wt(v, "mm")
                vfl(v.tensor_copy(out=rdsb[:, :], in_=pF[:, 0:128]))
                pdone("pF", "dve")
                inc(v.tensor_mul(out=lout[:, o:o + 128], in0=pA[:, 0:128], in1=rdsb[:, :]), "dve")
                pdone("pA", "dve")
                vfl(v.tensor_add(out=SST[:, :], in0=SST[:, :], in1=pD[:, 0:128]))
                pdone("pD", "dve")
                inc(v.tensor_copy(out=SSTB[:, :], in_=SST[:, :]), "dve")
                vfl(v.tensor_reduce(out=ZST[:, 1:2], in_=kP[:, o:o + 128], axis=AX.X, op=ALU.add))
                vfl(v.tensor_add(out=ZST[:, 0:1], in0=ZST[:, 0:1], in1=ZST[:, 1:2]))
                inc(v.tensor_copy(out=ZSTB[:, 0:1], in_=ZST[:, 0:1]), "dve")
            wt(t, "dve")
            pguard("pE")
            inc(t.matmul(pE[:, 0:1], ONESRF[0:1, :], GROW[0:1, 8 + gl:9 + gl], start=True, stop=True), "mm")
            wt(v, "mm")
            vfl(v.tensor_copy(out=GCOL[:, 7:8], in_=pE[:, 0:1]))
            pdone("pE", "dve")
            wt(v, "st")
            inc(v.scalar_tensor_tensor(out=oout[:, :], in0=lout[:, :], scalar=GCOL[:, 7:8],
                                       in1=maccb[:, :], op0=ALU.mult, op1=ALU.add), "dve")
            wt(q, "dve")
            incd(q.dma_start(out=amixL[gl * 128:(gl + 1) * 128, :], in_=oout[:, :]), "st")
            if PARANOID & 8:
                barrier()
        barrier()
        oproj_partial(woTmy, amixL)
        barrier()

    def emit_ffn():
        if PARANOID & 1:
            barrier()
        emit_ln()
        if PARANOID & 2:
            barrier()
        h = BIG.bitcast(BF16)
        WU = (WS[:, 0:1024].bitcast(BF16), WS[:, 1024:2048].bitcast(BF16))
        WG = (WS[:, 2048:3072].bitcast(BF16), WS[:, 3072:4096].bitcast(BF16))
        for th in range(2):
            wt(t, "dve")
            mm_buf = {}
            for ec in range(64):
                b = ec % 2
                if ec >= 2:
                    wt(q, "mm", mm_buf[ec - 2])
                wt(q, "cc", cc_tern)
                ws_ = "w%d" % b
                incd(q.dma_start(out=WU[b].rearrange("p (dc e) -> p dc e", dc=NC16),
                                 in_=wu_q.rearrange("(dc p) e -> p dc e", p=128)[:, :, ec * 128:(ec + 1) * 128]),
                     ws_)
                incd(q.dma_start(out=WG[b].rearrange("p (dc e) -> p dc e", dc=NC16),
                                 in_=wg_q.rearrange("(dc p) e -> p dc e", p=128)[:, :, ec * 128:(ec + 1) * 128]),
                     ws_)
                wt(t, ws_)
                psu = (pA, pB)[b]
                psg = (pC, pD)[b]
                pguard(PN[id(psu)])
                pguard(PN[id(psg)])
                for dc in range(NC16):
                    inc(t.matmul(psu[:, :], WU[b][:, dc * 128:(dc + 1) * 128],
                                 XL[:, dc * T + th * 512: dc * T + th * 512 + 512],
                                 start=(dc == 0), stop=(dc == NC16 - 1)), "mm")
                for dc in range(NC16):
                    inc(t.matmul(psg[:, :], WG[b][:, dc * 128:(dc + 1) * 128],
                                 XL[:, dc * T + th * 512: dc * T + th * 512 + 512],
                                 start=(dc == 0), stop=(dc == NC16 - 1)), "mm")
                mm_buf[ec] = cnt["mm"]
                wt(sc, "mm")
                hs = QSQ[:, (b) * 512:(b) * 512 + 512]
                inc(sc.activation(out=hs, in_=psu[:, :], func=AF.Silu, scale=SCOL[:, 0:1]), "act")
                pdone(PN[id(psu)], "act")
                wt(v, "act")
                inc(v.scalar_tensor_tensor(out=h[:, ec * 512:(ec + 1) * 512], in0=psg[:, :],
                                           scalar=SCOL[:, 1:2], in1=hs, op0=ALU.mult, op1=ALU.mult), "dve")
                pdone(PN[id(psg)], "dve")
            wt(q, "mm")
            wt(t, "dve")
            mm_d = {}
            for dc in range(NC16):
                psd = (pE, pF)[dc % 2]
                pguard(PN[id(psd)])
                for qa in range(4):
                    b = qa % 2
                    key = (dc, qa)
                    prev = (dc, qa - 2) if qa >= 2 else ((dc - 1, qa + 2) if dc >= 1 else None)
                    if prev is not None and prev in mm_d:
                        wt(q, "mm", mm_d[prev])
                    incd(q.dma_start(out=WU[b].rearrange("p (dc e) -> p dc e", dc=NC16),
                                     in_=wd_q.rearrange("(ea p) d -> p ea d", p=128)[:, qa * 16:(qa + 1) * 16,
                                                                                     dc * 128:(dc + 1) * 128]),
                         "w%d" % b)
                    wt(t, "w%d" % b)
                    for ea in range(16):
                        inc(t.matmul(psd[:, :], WU[b][:, ea * 128:(ea + 1) * 128],
                                     h[:, (qa * 16 + ea) * 512:(qa * 16 + ea) * 512 + 512],
                                     start=(qa == 0 and ea == 0), stop=(qa == 3 and ea == 15)), "mm")
                    mm_d[key] = cnt["mm"]
                wt(v, "mm")
                rsl = R[:, dc * T + th * 512: dc * T + th * 512 + 512]
                inc(v.scalar_tensor_tensor(out=rsl, in0=psd[:, :], scalar=SCOL[:, 2:3],
                                           in1=rsl, op0=ALU.mult, op1=ALU.add), "dve")
                pdone(PN[id(psd)], "dve")
            barrier()

    if "q_dbg" in stages:
        wt(q, "cc", cc_tern)
        wt(v, "cc", cc_tern)
        wt(g, "cc", cc_tern)
        inc(v.tensor_copy(out=R[0:1, 0:64], in_=SROW[0:1, 0:64]), "dve")
        inc(v.tensor_copy(out=R[0:128, 64:76], in_=SCOL[:, 0:12]), "dve")
        incd(q.dma_start(out=R.rearrange("p (c t) -> p c t", c=NC16)[:, 1, 0:1024],
                         in_=wu_q[0:128, 0:1024]), "ld")
        incd(q.dma_start(out=R.rearrange("p (c t) -> p c t", c=NC16)[:, 2, 0:1024],
                         in_=wd_q[0:128, 0:1024]), "ld")
        wt(q, "ld")
        barrier()

    if "ln_dbg" in stages:
        emit_ln()
        wt(v, "mm")
        for dc in range(NC16):
            inc(v.tensor_copy(out=R[:, dc * T:(dc + 1) * T], in_=XL[:, dc * T:(dc + 1) * T]), "dve")

    if "ffn1" in stages:
        emit_ffn()

    if "attn_b" in stages:
        emit_ln()
        ag_xln()
        barrier()
        QF = BIG[:, 0:8192].bitcast(BF16)
        mm_w = {}
        for ec in range(NC16):
            b = ec % 2
            wb = WS[:, b * 1024:(b + 1) * 1024].bitcast(BF16)
            if ec >= 2:
                wt(g, "mm", mm_w[ec - 2])
            incd(g.dma_start(out=wb.rearrange("p (dc e) -> p dc e", dc=NC16),
                             in_=bWq.rearrange("(dc p) e -> p dc e", p=128)[:, :, ec * 128:(ec + 1) * 128]),
                 "g%d" % b)
            wt(t, "g%d" % b)
            for th in range(2):
                ps = (pA, pB)[th]
                pguard(PN[id(ps)])
                for dc in range(NC16):
                    inc(t.matmul(ps[:, :], wb[:, dc * 128:(dc + 1) * 128],
                                 XL[:, dc * T + th * 512: dc * T + th * 512 + 512],
                                 start=(dc == 0), stop=(dc == NC16 - 1)), "mm")
                mm_w[ec] = cnt["mm"]
                wt(sc, "mm")
                inc(sc.activation(out=QF[:, ec * T + th * 512: ec * T + th * 512 + 512],
                                  in_=ps[:, :], func=AF.Identity, bias=BIAS[:, ec:ec + 1]), "act")
                pdone(PN[id(ps)], "act")
        wt(t, "act")
        barrier()
        proj_heads((bWkmy, bWvmy), bkvL, BIG[:, 8192:16384].bitcast(BF16),
                   bias_of=lambda wi, ec: BIAS[:, 16 + wi * 8 + ec: 16 + wi * 8 + ec + 1])
        wt_st(q)
        kF = BIG[:, 12288:13312].bitcast(BF16)
        vF = BIG[:, 13312:14336].bitcast(BF16)
        vT = BIG[:, 14336:15360].bitcast(BF16)
        outst = BIG[:, 15360:15616].bitcast(BF16)
        denb = BIG[:, 15616:16128]
        PT = BIG[:, 8192:12288].bitcast(BF16)
        QF3 = QF.rearrange("p (m tk) -> p tk m", m=16)
        for gl in range(NHL):
            wt(q, "dve")
            wt_st(q)
            wt(q, "act")
            incd(q.dma_start(out=kF[:, :], in_=bkvL[0, gl * 128:(gl + 1) * 128, :]), "ld")
            incd(q.dma_start(out=vF[:, :], in_=bkvL[1, gl * 128:(gl + 1) * 128, :]), "ld")
            wt(t, "ld")
            wt(v, "ld")
            for ch in range(16):
                ps = (pC, pD)[ch % 2]
                psb = ps[:, 0:64].bitcast(BF16)
                pguard(PN[id(ps)])
                inc(t.transpose(psb, vF[:, ch * 128:(ch + 1) * 128], IDT[:, :]), "mm")
                wt(v, "mm")
                inc(v.tensor_copy(out=vT[:, ch * 128:(ch + 1) * 128], in_=psb), "dve")
                pdone(PN[id(ps)], "dve")
            vfl(v.memset(SROW[0:1, 50:52], 0.0))
            for hf in range(2):
                inc(v.tensor_mul(out=QSQ.rearrange("p (tk m) -> p tk m", m=8),
                                 in0=QF3[:, gl * 128:(gl + 1) * 128, hf * 8:(hf + 1) * 8],
                                 in1=QF3[:, gl * 128:(gl + 1) * 128, hf * 8:(hf + 1) * 8]), "dve")
                pguard("pS1")
                for m4 in range(2):
                    wt(t, "dve")
                    inc(t.matmul(pS1[0:1, :], ONESB[:, 0:1], QSQ[:, m4 * 512:(m4 + 1) * 512],
                                 start=True, stop=True), "mm")
                    wt(v, "mm")
                    vfl(v.tensor_reduce(out=SROW[0:1, 52:53], in_=pS1[0:1, :], axis=AX.X, op=ALU.max))
                    vfl(v.tensor_tensor(out=SROW[0:1, 50:51], in0=SROW[0:1, 50:51],
                                        in1=SROW[0:1, 52:53], op=ALU.max))
                pdone("pS1", "dve")
            for hf in range(2):
                inc(v.tensor_mul(out=QSQ[:, :], in0=kF[:, hf * 1024:(hf + 1) * 1024],
                                 in1=kF[:, hf * 1024:(hf + 1) * 1024]), "dve")
                pguard("pS1")
                for m4 in range(2):
                    wt(t, "dve")
                    inc(t.matmul(pS1[0:1, :], ONESB[:, 0:1], QSQ[:, m4 * 512:(m4 + 1) * 512],
                                 start=True, stop=True), "mm")
                    wt(v, "mm")
                    vfl(v.tensor_reduce(out=SROW[0:1, 53:54], in_=pS1[0:1, :], axis=AX.X, op=ALU.max))
                    vfl(v.tensor_tensor(out=SROW[0:1, 51:52], in0=SROW[0:1, 51:52],
                                        in1=SROW[0:1, 53:54], op=ALU.max))
                pdone("pS1", "dve")
            vfl(v.tensor_add(out=SROW[0:1, 54:55], in0=SROW[0:1, 50:51], in1=SROW[0:1, 51:52]))
            inc(v.tensor_scalar_mul(out=SROW[0:1, 54:55], in0=SROW[0:1, 54:55],
                                    scalar1=-0.51 * SCALE), "dve")
            wt(t, "dve")
            pguard("pS2")
            inc(t.matmul(pS2[:, 0:1], ONESRF[0:1, :], SROW[0:1, 54:55], start=True, stop=True), "mm")
            wt(v, "mm")
            inc(v.tensor_copy(out=GCOL[:, 5:6], in_=pS2[:, 0:1]), "dve")
            pdone("pS2", "dve")
            wt(sc, "dve")
            for jt in range(4):
                na = 4 * jt + 4
                rhs = QF3[:, gl * 128 + 32 * jt: gl * 128 + 32 * jt + 32, :]
                for a in range(na):
                    ps = (pC, pD)[a % 2]
                    pguard(PN[id(ps)])
                    inc(t.matmul(ps[:, :], kF[:, a * 128:(a + 1) * 128], rhs, start=True, stop=True), "mm")
                    wt(sc, "mm")
                    inc(sc.activation(out=PT[:, a * 512:(a + 1) * 512], in_=ps[:, :], func=AF.Exp,
                                      bias=GCOL[:, 5:6], scale=SCALE), "act")
                    pdone(PN[id(ps)], "act")
                wt(v, "act")
                for r in range(4):
                    a = 4 * jt + r
                    inc(v.tensor_mul(out=PT[:, a * 512:(a + 1) * 512], in0=PT[:, a * 512:(a + 1) * 512],
                                     in1=BMASK[:, r * 512:(r + 1) * 512]), "dve")
                wt(t, "dve")
                pguard("pS1")
                for a in range(na):
                    inc(t.matmul(pS1[0:1, :], ONESB[:, 0:1], PT[:, a * 512:(a + 1) * 512],
                                 start=(a == 0), stop=(a == na - 1)), "mm")
                wt(v, "mm")
                vfl(v.tensor_single_scalar(out=P0[0:1, 0:512], in_=pS1[0:1, :], scalar=1e-30,
                                           op=ALU.max))
                inc(v.reciprocal(out=P0[0:1, 0:512], in_=P0[0:1, 0:512]), "dve")
                pdone("pS1", "dve")
                wt(t, "dve")
                pguard("pS2")
                inc(t.matmul(pS2[:, :], ONESRF[0:1, :], P0[0:1, 0:512], start=True, stop=True), "mm")
                wt(v, "mm")
                vfl(v.tensor_copy(out=denb[:, :], in_=pS2[:, :]))
                pdone("pS2", "dve")
                pguard("pA")
                for a in range(na):
                    inc(t.matmul(pA[:, :], vT[:, a * 128:(a + 1) * 128], PT[:, a * 512:(a + 1) * 512],
                                 start=(a == 0), stop=(a == na - 1)), "mm")
                wt(v, "mm")
                if st_prev.get("bo"):
                    wt(v, "st", st_prev["bo"])
                inc(v.tensor_mul(out=outst[:, :], in0=pA[:, :], in1=denb[:, :]), "dve")
                pdone("pA", "dve")
                wt(q, "dve")
                st_prev["bo"] = incd(q.dma_start(out=bmixL[gl * 128:(gl + 1) * 128,
                                                           jt * 512:(jt + 1) * 512],
                                                 in_=outst[:, :]), "st")
            if PARANOID & 4:
                barrier()
        barrier()
        oproj_partial(bWomy, bmixL, bias_idx=32)
        barrier()

    if "ffn2" in stages:
        emit_ffn()

    wt(q, "dve")
    incd(q.dma_start(out=outT.rearrange("(c p) t -> p c t", p=128),
                     in_=R.rearrange("p (c t) -> p c t", c=NC16)), "st")
    wt_st(q)

    es.close()
    return nc


def _mk_consts():
    ident = np.eye(128, dtype=np.float32)
    triu = np.triu(np.ones((128, 128), dtype=np.float32))
    bmask = np.zeros((128, 4, 512), dtype=np.float32)
    sl = np.arange(128)[:, None]
    jl = np.arange(512)[None, :]
    for r in range(4):
        bmask[:, r, :] = (sl <= (jl - 128 * r)).astype(np.float32)
    ones_c = np.ones((128, 2), dtype=np.float32)
    ones_r = np.ones((1, 128), dtype=np.float32)
    sel4 = np.zeros((4, 4, 128), dtype=np.float32)
    for m in range(4):
        sel4[m, m, :] = 1.0
    return ident, triu, bmask.reshape(128, 2048), ones_c, ones_r, sel4.reshape(4, 512)


_NC_CACHE = {}


def _get_nc(stages):
    key = tuple(stages)
    if key not in _NC_CACHE:
        _NC_CACHE[key] = build(stages)
    return _NC_CACHE[key]


def kernel(L0_attn_a_hidden_states, L0_attn_a_memories, L0_attn_a_memory_norms,
           L0_attn_a_w_q, L0_attn_a_w_k, L0_attn_a_w_v, L0_attn_a_w_o, L0_attn_a_gate,
           L0_attn_b_Wq, L0_attn_b_bq, L0_attn_b_Wk, L0_attn_b_bk, L0_attn_b_Wv,
           L0_attn_b_bv, L0_attn_b_Wo, L0_attn_b_bo, L0_ffn_W_up, L0_ffn_W_gate,
           L0_ffn_W_down, _stages=ALL_STAGES):
    x = np.ascontiguousarray(np.asarray(L0_attn_a_hidden_states, dtype=np.float32))
    B = x.shape[0]
    mem = np.asarray(L0_attn_a_memories, dtype=np.float32)
    mnorm = np.asarray(L0_attn_a_memory_norms, dtype=np.float32)
    awq = np.asarray(L0_attn_a_w_q, dtype=np.float32)
    awk = np.asarray(L0_attn_a_w_k, dtype=np.float32)
    awv = np.asarray(L0_attn_a_w_v, dtype=np.float32)
    awo = np.asarray(L0_attn_a_w_o, dtype=np.float32)
    gate = np.asarray(L0_attn_a_gate, dtype=np.float32)
    bwq = np.ascontiguousarray(np.asarray(L0_attn_b_Wq, dtype=np.float32))
    bwk = np.asarray(L0_attn_b_Wk, dtype=np.float32)
    bwv = np.asarray(L0_attn_b_Wv, dtype=np.float32)
    bwo = np.asarray(L0_attn_b_Wo, dtype=np.float32)
    bq_ = np.asarray(L0_attn_b_bq, dtype=np.float32)
    bk_ = np.asarray(L0_attn_b_bk, dtype=np.float32)
    bv_ = np.asarray(L0_attn_b_bv, dtype=np.float32)
    bo_ = np.asarray(L0_attn_b_bo, dtype=np.float32)
    wu = np.asarray(L0_ffn_W_up, dtype=np.float32)
    wg = np.asarray(L0_ffn_W_gate, dtype=np.float32)
    wd = np.asarray(L0_ffn_W_down, dtype=np.float32)

    awqT = np.ascontiguousarray(awq.T)
    awkT = np.ascontiguousarray(awk.T)
    awvT = np.ascontiguousarray(awv.T)
    awoT = np.ascontiguousarray(awo.T)
    bwoT = np.ascontiguousarray(bwo)
    wuT = np.ascontiguousarray(wu.T)
    wgT = np.ascontiguousarray(wg.T)
    wdT = np.ascontiguousarray(wd.T)
    ident, triu, bmask, ones_c, ones_r, sel4 = _mk_consts()
    mnF = np.ascontiguousarray(mnorm.transpose(2, 0, 1).reshape(128, 64))

    in_maps = []
    for c in range(8):
        b, hg = divmod(c, 2)
        heads = slice(8 * hg, 8 * hg + 8)
        chs = slice(1024 * hg, 1024 * hg + 1024)
        im = {
            "xT": np.ascontiguousarray(x[b, 1024 * hg:1024 * hg + 1024, :].T),
            "wqTmy": np.ascontiguousarray(awqT[:, chs]),
            "wkTmy": np.ascontiguousarray(awkT[:, chs]),
            "wvTmy": np.ascontiguousarray(awvT[:, chs]),
            "woTmy": np.ascontiguousarray(awoT[chs, :]),
            "memsI": np.ascontiguousarray(mem[:, heads, :, :]),
            "mnT": np.ascontiguousarray(mnorm[:, heads, :].transpose(1, 2, 0)),
            "mnF": mnF,
            "gateL": np.ascontiguousarray(gate[heads].reshape(1, 8)),
            "bWq": bwq,
            "bWkmy": np.ascontiguousarray(bwk[:, chs]),
            "bWvmy": np.ascontiguousarray(bwv[:, chs]),
            "bWomy": np.ascontiguousarray(bwoT[chs, :]),
            "bbq": np.ascontiguousarray(bq_.reshape(16, 128).T),
            "bbkv": np.ascontiguousarray(np.concatenate(
                [bk_[1024 * hg:1024 * hg + 1024].reshape(8, 128).T,
                 bv_[1024 * hg:1024 * hg + 1024].reshape(8, 128).T], axis=1)),
            "bbo": np.ascontiguousarray(bo_.reshape(16, 128).T),
            "wuS": np.ascontiguousarray(wuT[256 * c:256 * (c + 1), :]),
            "wgS": np.ascontiguousarray(wgT[256 * c:256 * (c + 1), :]),
            "wdS": np.ascontiguousarray(wdT[1024 * c:1024 * (c + 1), :]),
            "identC": ident,
            "sel4C": sel4,
            "triuC": triu,
            "bmaskC": bmask,
            "onesC": ones_c,
            "onesR": ones_r,
        }
        in_maps.append(im)

    nc = _get_nc(_stages)
    res = run_bass_kernel_spmd(nc, in_maps, core_ids=list(range(8)))
    out = np.empty((B, 2048, 2048), dtype=np.float32)
    for c in range(8):
        b, hg = divmod(c, 2)
        out[b, 1024 * hg:1024 * hg + 1024, :] = res.results[c]["outT"].T
    return out


if __name__ == "__main__":
    import reference as _r
    inp = _r.setup_inputs(0)
    outs = kernel(**{k: np.asarray(v) for k, v in inp.items()})
    ref = np.asarray(_r.reference(**inp))
    err = np.abs(outs - ref)
    denom = np.sqrt(np.mean(ref ** 2))
    print("rel_l2:", np.sqrt(np.mean((outs - ref) ** 2)) / denom, "max_abs:", err.max())


# version 12
# speedup vs baseline: 1.0183x; 1.0183x over previous
import contextlib
import numpy as np

import concourse.bass as bass
import concourse.mybir as mybir
from concourse.bass_utils import run_bass_kernel_spmd

F32 = mybir.dt.float32
BF16 = mybir.dt.bfloat16
AF = mybir.ActivationFunctionType
ALU = mybir.AluOpType
AX = mybir.AxisListType

D = 2048
NC16 = 16
S = 2048
T = 1024
HD = 128
NHL = 8
M = 4
E = 8192
SCALE = float(HD) ** -0.5
EPS_MEM = 1e-6
EPS_LN = 1e-5
EPS_Q = 1e-8
WCNT = float(D * E)
PAIRS = [[0, 1], [2, 3], [4, 5], [6, 7]]
ALL8 = [list(range(8))]

ALL_STAGES = ("attn_a", "ffn1", "attn_b", "ffn2")
PARANOID = 0


def build(stages=ALL_STAGES):
    nc = bass.Bass("TRN2", target_bir_lowering=False, debug=False, num_devices=8)
    es = contextlib.ExitStack()
    sems = {}
    cnt = {}
    bank_free = {}

    def sem(nm):
        sems[nm] = es.enter_context(nc.semaphore(nm))
        cnt[nm] = 0

    def inc(bi, nm, k=1):
        bi.then_inc(sems[nm], k)
        cnt[nm] += k
        return cnt[nm]

    def incd(bi, nm):
        return inc(bi, nm, 16)

    def wt(eng, nm, val=None):
        eng.wait_ge(sems[nm], cnt[nm] if val is None else val)

    def pguard(bank):
        ent = bank_free.get(bank)
        if ent is not None:
            wt(t, ent[0], ent[1])

    def pdone(bank, nm):
        bank_free[bank] = (nm, cnt[nm])

    def wt_st(eng):
        for nm in ("st", "sb0", "sb1", "sf0", "sf1"):
            wt(eng, nm)

    def vfl(bi):
        n = inc(bi, "dve")
        nc.vector.wait_ge(sems["dve"], n)
        return n

    def sfl(bi):
        n = inc(bi, "act")
        nc.scalar.wait_ge(sems["act"], n)
        return n

    def DI(n, sh):
        return nc.dram_tensor(n, sh, F32, kind="ExternalInput").ap()

    xT = DI("xT", [D, T])
    outT = nc.dram_tensor("outT", [D, T], F32, kind="ExternalOutput").ap()
    wqTmy = DI("wqTmy", [D, T])
    wkTmy = DI("wkTmy", [D, T])
    wvTmy = DI("wvTmy", [D, T])
    woTmy = DI("woTmy", [T, D])
    memsI = DI("memsI", [M, NHL, HD, HD])
    mnT = DI("mnT", [NHL, HD, M])
    mnF = DI("mnF", [HD, M * 16])
    gateL = DI("gateL", [1, NHL])
    bWq = DI("bWq", [D, D])
    bWkmy = DI("bWkmy", [D, T])
    bWvmy = DI("bWvmy", [D, T])
    bWomy = DI("bWomy", [T, D])
    bbq = DI("bbq", [128, NC16])
    bbkv = DI("bbkv", [128, NHL * 2])
    bbo = DI("bbo", [128, NC16])
    wuS = DI("wuS", [D, 1024])
    wgS = DI("wgS", [D, 1024])
    wdS = DI("wdS", [1024, D])
    identC = DI("identC", [128, 128])
    sel4C = DI("sel4C", [4, 512])
    triuC = DI("triuC", [128, 128])
    bmaskC = DI("bmaskC", [128, 4 * 512])
    onesC = DI("onesC", [128, 2])
    onesR = DI("onesR", [1, 128])

    tern_u = nc.dram_tensor("tern_u", [8, 128, 16, 128], BF16).ap()
    tern_g = nc.dram_tensor("tern_g", [8, 128, 16, 128], BF16).ap()
    tern_d = nc.dram_tensor("tern_d", [16, 128, 8, 128], BF16).ap()
    wu_q = nc.dram_tensor("wu_q", [64, 128, 16, 128], BF16).ap()
    wg_q = nc.dram_tensor("wg_q", [64, 128, 16, 128], BF16).ap()
    wd_q = nc.dram_tensor("wd_q", [128, 128, 8, 128], BF16).ap()
    sc_in = nc.dram_tensor("sc_in", [1, 4], F32).ap()
    sc_out = nc.dram_tensor("sc_out", [1, 4], F32).ap()
    xag_in = nc.dram_tensor("xag_in", [D, T], BF16).ap()
    xag_out = nc.dram_tensor("xag_out", [2, D, T], BF16).ap()
    aqkvL = nc.dram_tensor("aqkvL", [3, T, S], BF16).ap()
    amixL = nc.dram_tensor("amixL", [T, S], BF16).ap()
    ops_in = nc.dram_tensor("ops_in", [2, D, T], F32).ap()
    ops_out = nc.dram_tensor("ops_out", [D, T], F32).ap()
    bkvL = nc.dram_tensor("bkvL", [2, T, S], BF16).ap()
    bmixL = nc.dram_tensor("bmixL", [T, S], BF16).ap()

    R = es.enter_context(nc.sbuf_tensor("R", [128, NC16 * T], F32))
    AR = es.enter_context(nc.sbuf_tensor("AR", [128, 28672], F32))
    XL = AR[:, 0:8192].bitcast(BF16)
    WS = AR[:, 8192:12288]
    BIG = AR[:, 12288:28672]

    IDT = es.enter_context(nc.sbuf_tensor("IDT", [128, 128], BF16))
    IDF = es.enter_context(nc.sbuf_tensor("IDF", [128, 4], F32))
    TRIU = es.enter_context(nc.sbuf_tensor("TRIU", [128, 128], BF16))
    BMASK = es.enter_context(nc.sbuf_tensor("BMASK", [128, 4 * 512], BF16))
    ONESB = es.enter_context(nc.sbuf_tensor("ONESB", [128, 2], BF16))
    ONESF = es.enter_context(nc.sbuf_tensor("ONESF", [128, 2], F32))
    ONESRF = es.enter_context(nc.sbuf_tensor("ONESRF", [1, 128], F32))
    BIAS = es.enter_context(nc.sbuf_tensor("BIAS", [128, 48], F32))
    MNT = es.enter_context(nc.sbuf_tensor("MNT", [128, NHL * M], BF16))
    MNF = es.enter_context(nc.sbuf_tensor("MNF", [128, 64], F32))
    GROW = es.enter_context(nc.sbuf_tensor("GROW", [1, 64], F32))
    SCOL = es.enter_context(nc.sbuf_tensor("SCOL", [128, 12], F32))
    P0 = es.enter_context(nc.sbuf_tensor("P0", [1, 2048], F32))
    SEL4 = es.enter_context(nc.sbuf_tensor("SEL4", [4, 512], F32))
    SROW = es.enter_context(nc.sbuf_tensor("SROW", [1, 64], F32))
    GCOL = es.enter_context(nc.sbuf_tensor("GCOL", [128, 8], F32))
    SST = es.enter_context(nc.sbuf_tensor("SST", [128, 128], F32))
    SSTB = es.enter_context(nc.sbuf_tensor("SSTB", [128, 128], BF16))
    ZST = es.enter_context(nc.sbuf_tensor("ZST", [128, 4], F32))
    ZSTB = es.enter_context(nc.sbuf_tensor("ZSTB", [128, 2], BF16))
    MEMW = es.enter_context(nc.sbuf_tensor("MEMW", [128, M * HD], BF16))
    TMP = es.enter_context(nc.sbuf_tensor("TMP", [128, 512], F32))
    TMP2 = es.enter_context(nc.sbuf_tensor("TMP2", [128, 512], F32))
    OST = es.enter_context(nc.sbuf_tensor("OST", [128, 1024], F32))
    QSQ = es.enter_context(nc.sbuf_tensor("QSQ", [128, 1024], BF16))
    SP = es.enter_context(nc.sbuf_tensor("SP", [128, 48], F32))

    pA = es.enter_context(nc.psum_tensor("pA", [128, 512], F32))
    pB = es.enter_context(nc.psum_tensor("pB", [128, 512], F32))
    pC = es.enter_context(nc.psum_tensor("pC", [128, 512], F32))
    pD = es.enter_context(nc.psum_tensor("pD", [128, 512], F32))
    pE = es.enter_context(nc.psum_tensor("pE", [128, 512], F32))
    pF = es.enter_context(nc.psum_tensor("pF", [128, 512], F32))
    pS1 = es.enter_context(nc.psum_tensor("pS1", [128, 512], F32))
    pS2 = es.enter_context(nc.psum_tensor("pS2", [128, 512], F32))
    PN = {id(pA): "pA", id(pB): "pB", id(pC): "pC", id(pD): "pD",
          id(pE): "pE", id(pF): "pF", id(pS1): "pS1", id(pS2): "pS2"}

    for nm in ("ld", "ldw", "gw", "gld", "st", "mm", "dve", "act", "cc",
               "q0", "q1", "w0", "w1", "g0", "g1", "sb0", "sb1", "sf0", "sf1",
               "bt", "bv", "bs", "bq", "bg"):
        sem(nm)

    t = nc.tensor
    v = nc.vector
    sc = nc.scalar
    q = nc.sync
    g = nc.gpsimd

    def barrier():
        for e, nm in ((t, "bt"), (v, "bv"), (sc, "bs"), (q, "bq"), (g, "bg")):
            inc(e.nop(), nm)
        for e in (t, v, sc, q, g):
            for nm in ("bt", "bv", "bs", "bq", "bg"):
                wt(e, nm)

    incd(g.dma_start(out=IDT[:, :], in_=identC[:, :]), "gld")
    incd(q.dma_start(out=IDF[:, 0:4], in_=identC[:, 0:4]), "ld")
    incd(q.dma_start(out=SEL4[:, :], in_=sel4C[:, :]), "ld")
    incd(g.dma_start(out=TRIU[:, :], in_=triuC[:, :]), "gld")
    incd(g.dma_start(out=BMASK[:, :], in_=bmaskC[:, :]), "gld")
    incd(g.dma_start(out=ONESB[:, :], in_=onesC[:, :]), "gld")
    incd(q.dma_start(out=ONESF[:, :], in_=onesC[:, :]), "ld")
    incd(q.dma_start(out=ONESRF[:, :], in_=onesR[:, :]), "ld")
    incd(g.dma_start(out=MNT.rearrange("p (h m) -> p h m", h=NHL),
                     in_=mnT.transpose([1, 0, 2])), "gld")
    incd(q.dma_start(out=MNF[:, :], in_=mnF[:, :]), "ld")
    incd(q.dma_start(out=GROW[0:1, 0:NHL], in_=gateL[0:1, :]), "ld")
    incd(q.dma_start(out=BIAS[:, 0:16], in_=bbq[:, :]), "ld")
    incd(q.dma_start(out=BIAS[:, 16:32], in_=bbkv[:, :]), "ld")
    incd(q.dma_start(out=BIAS[:, 32:48], in_=bbo[:, :]), "ld")
    incd(q.dma_start(out=R.rearrange("p (c t) -> p c t", c=NC16),
                     in_=xT.rearrange("(c p) t -> p c t", p=128)), "ld")
    wt(sc, "ld")
    wt(sc, "gld")
    inc(sc.activation(out=GROW[0:1, 16:24], in_=GROW[0:1, 0:8], func=AF.Sigmoid), "act")
    wt(v, "act")
    inc(v.tensor_scalar(out=GROW[0:1, 8:16], in0=GROW[0:1, 16:24], scalar1=-1.0, scalar2=1.0,
                        op0=ALU.mult, op1=ALU.add), "dve")
    inc(v.tensor_reduce(out=GCOL[:, 0:4], in_=MNF.rearrange("p (m h) -> p m h", m=M),
                        axis=AX.X, op=ALU.add), "dve")
    wt(t, "dve")
    pguard("pS1")
    inc(t.matmul(pS1[0:1, 0:4], ONESF[:, 0:1], GCOL[:, 0:4], start=True, stop=True), "mm")
    wt(v, "mm")
    inc(v.tensor_single_scalar(out=SROW[0:1, 60:64], in_=pS1[0:1, 0:4], scalar=EPS_MEM,
                               op=ALU.is_ge), "dve")
    pdone("pS1", "dve")

    WSL = ((wuS, 16, 1024, tern_u, 0), (wgS, 16, 1024, tern_g, 1), (wdS, 8, D, tern_d, 2))
    WSb = (WS[:, 0:1024], WS[:, 1024:2048])
    dve_n = {}
    ia = 0
    for (wsl, nch, width, _td, i) in WSL:
        for it in range(nch * (width // 1024)):
            buf = WSb[ia % 2]
            ch, off = divmod(it, width // 1024)
            if ia >= 2:
                wt(q, "dve", dve_n[ia - 2])
            qs = "q%d" % (ia % 2)
            incd(q.dma_start(out=buf[:, :], in_=wsl[ch * 128:(ch + 1) * 128,
                                                    off * 1024:(off + 1) * 1024]), qs)
            wt(v, qs)
            dve_n[ia] = inc(v.tensor_reduce(out=SP[:, ia:ia + 1], in_=buf[:, :],
                                            axis=AX.X, op=ALU.add,
                                            apply_absolute_value=True), "dve")
            ia += 1
    wt(v, "dve")
    for i in range(3):
        inc(v.tensor_reduce(out=GCOL[:, 4 + i:5 + i], in_=SP[:, 16 * i:16 * i + 16],
                            axis=AX.X, op=ALU.add), "dve")
    wt(t, "dve")
    pguard("pS2")
    inc(t.matmul(pS2[0:1, 0:3], ONESF[:, 0:1], GCOL[:, 4:7], start=True, stop=True), "mm")
    wt(v, "mm")
    inc(v.tensor_copy(out=SROW[0:1, 0:3], in_=pS2[0:1, 0:3]), "dve")
    pdone("pS2", "dve")
    inc(v.memset(SROW[0:1, 3:4], 0.0), "dve")
    wt(q, "dve")
    incd(q.dma_start(out=sc_in[0:1, 0:4], in_=SROW[0:1, 0:4]), "st")
    wt(g, "st")
    inc(g.collective_compute("AllReduce", ALU.add, replica_groups=ALL8,
                             ins=[sc_in[:, :]], outs=[sc_out[:, :]]), "cc")
    wt(q, "cc")
    incd(q.dma_start(out=SROW[0:1, 4:8], in_=sc_out[0:1, 0:4]), "ld")
    wt(v, "ld")
    vfl(v.tensor_scalar(out=SROW[0:1, 8:11], in0=SROW[0:1, 4:7], scalar1=1.0 / WCNT,
                        scalar2=EPS_Q, op0=ALU.mult, op1=ALU.add))
    vfl(v.tensor_scalar_mul(out=SROW[0:1, 12:15], in0=SROW[0:1, 8:11], scalar1=0.5))
    vfl(v.tensor_scalar_mul(out=SROW[0:1, 16:19], in0=SROW[0:1, 12:15], scalar1=-1.0))
    wt(t, "dve")
    pguard("pS2")
    inc(t.matmul(pS2[:, 0:3], ONESRF[0:1, :], SROW[0:1, 12:15], start=True, stop=True), "mm")
    inc(t.matmul(pS2[:, 4:7], ONESRF[0:1, :], SROW[0:1, 16:19], start=True, stop=True), "mm")
    wt(v, "mm")
    inc(v.tensor_copy(out=SCOL[:, 0:3], in_=pS2[:, 0:3]), "dve")
    vfl(v.tensor_copy(out=SCOL[:, 4:7], in_=pS2[:, 4:7]))
    pdone("pS2", "dve")
    inc(v.tensor_scalar_mul(out=SCOL[:, 8:11], in0=SCOL[:, 4:7], scalar1=-1.0), "dve")
    s1b = BIG[:, 0:512].bitcast(BF16)
    s2b = BIG[:, 512:1024].bitcast(BF16)
    sob = BIG[:, 1024:1536].bitcast(BF16)
    st_n = {}
    ia = 0
    wt(sc, "dve")
    for (wsl, nch, width, td, i) in WSL:
        for it in range(nch * (width // 1024)):
            buf = WSb[ia % 2]
            ch, off = divmod(it, width // 1024)
            if ia >= 2:
                wt(q, "act", st_n[ia - 2][1])
            qs = "q%d" % (ia % 2)
            incd(q.dma_start(out=buf[:, :], in_=wsl[ch * 128:(ch + 1) * 128,
                                                    off * 1024:(off + 1) * 1024]), qs)
            wt(sc, qs)
            if ia >= 1:
                wt(sc, "st", st_n[ia - 1][0])
            inc(sc.activation(out=s1b[:, :], in_=buf[:, :], func=AF.Sign, bias=SCOL[:, 4 + i:5 + i]), "act")
            a2 = inc(sc.activation(out=s2b[:, :], in_=buf[:, :], func=AF.Sign, bias=SCOL[:, 8 + i:9 + i]), "act")
            wt(v, "act")
            inc(v.tensor_add(out=sob[:, :], in0=s1b[:, :], in1=s2b[:, :]), "dve")
            wt(q, "dve")
            if i < 2:
                dst = td[:, :, ch, :].transpose([1, 0, 2])
            else:
                dst = td[8 * off:8 * off + 8, :, ch, :].transpose([1, 0, 2])
            st_n[ia] = (incd(q.dma_start(out=dst,
                                         in_=sob.rearrange("p (b e) -> p b e", b=8)), "st"), a2)
            ia += 1
    wt(g, "st")
    inc(g.collective_compute("AllGather", ALU.bypass, replica_groups=ALL8,
                             ins=[tern_u[:, :, :, :]], outs=[wu_q[:, :, :, :]]), "cc")
    inc(g.collective_compute("AllGather", ALU.bypass, replica_groups=ALL8,
                             ins=[tern_g[:, :, :, :]], outs=[wg_q[:, :, :, :]]), "cc")
    inc(g.collective_compute("AllGather", ALU.bypass, replica_groups=ALL8,
                             ins=[tern_d[:, :, :, :]], outs=[wd_q[:, :, :, :]]), "cc")
    cc_tern = cnt["cc"]
    barrier()

    def emit_ln():
        for th in range(2):
            to = th * 512
            pguard("pS1")
            pguard("pS2")
            for dc in range(NC16):
                rsl = R[:, dc * T + to: dc * T + to + 512]
                xsl = XL[:, dc * T + to: dc * T + to + 512]
                sq = QSQ[:, (dc % 2) * 512:(dc % 2) * 512 + 512]
                if dc >= 2:
                    wt(v, "mm")
                vfl(v.tensor_copy(out=xsl, in_=rsl))
                inc(v.tensor_mul(out=sq, in0=xsl, in1=xsl), "dve")
                wt(t, "dve")
                inc(t.matmul(pS1[0:1, :], ONESB[:, 0:1], xsl, start=(dc == 0), stop=(dc == NC16 - 1)), "mm")
                inc(t.matmul(pS2[0:1, :], ONESB[:, 0:1], sq, start=(dc == 0), stop=(dc == NC16 - 1)), "mm")
            wt(v, "mm")
            mean = P0[0:1, 0:512]
            var = P0[0:1, 512:1024]
            inv = P0[0:1, 1024:1536]
            nr = P0[0:1, 1536:2048]
            vfl(v.tensor_scalar_mul(out=mean, in0=pS1[0:1, :], scalar1=1.0 / D))
            vfl(v.tensor_scalar_mul(out=nr, in0=pS2[0:1, :], scalar1=1.0 / D))
            pdone("pS1", "dve")
            pdone("pS2", "dve")
            vfl(v.tensor_mul(out=var, in0=mean, in1=mean))
            vfl(v.scalar_tensor_tensor(out=var, in0=var, scalar=-1.0, in1=nr,
                                       op0=ALU.mult, op1=ALU.add))
            vfl(v.tensor_scalar_add(out=var, in0=var, scalar1=EPS_LN))
            inc(v.reciprocal(out=inv, in_=var), "dve")
            wt(sc, "dve")
            inc(sc.activation(out=inv, in_=inv, func=AF.Sqrt), "act")
            wt(v, "act")
            vfl(v.tensor_mul(out=nr, in0=inv, in1=inv))
            vfl(v.tensor_mul(out=nr, in0=nr, in1=var))
            vfl(v.tensor_scalar(out=nr, in0=nr, scalar1=-0.5, scalar2=1.5, op0=ALU.mult, op1=ALU.add))
            inc(v.tensor_mul(out=inv, in0=inv, in1=nr), "dve")
            wt(t, "dve")
            pguard("pE")
            pguard("pF")
            inc(t.matmul(pE[:, :], ONESRF[0:1, :], mean, start=True, stop=True), "mm")
            inc(t.matmul(pF[:, :], ONESRF[0:1, :], inv, start=True, stop=True), "mm")
            wt(v, "mm")
            for dc in range(NC16):
                rsl = R[:, dc * T + to: dc * T + to + 512]
                xsl = XL[:, dc * T + to: dc * T + to + 512]
                vfl(v.tensor_sub(out=TMP[:, :], in0=rsl, in1=pE[:, :]))
                vfl(v.tensor_mul(out=xsl, in0=TMP[:, :], in1=pF[:, :]))
            pdone("pE", "dve")
            pdone("pF", "dve")

    def cached_w_load(w_dram, n_dc, dst_bf):
        cols = w_dram.shape[1]
        n = None
        for dd in range(0, n_dc, 4):
            nn = min(4, n_dc - dd)
            src = w_dram.rearrange("(dc p) e -> p dc e", p=128)[:, dd:dd + nn, :]
            dstv = dst_bf.rearrange("p (dc e) -> p dc e", dc=n_dc)[:, dd:dd + nn, :]
            n = incd(g.dma_start(out=dstv, in_=src), "gw")
        return n

    def mm_cached(wbuf, n_dc, n_ec, rhs_fn, sink):
        cols = n_ec * 128
        for ec in range(n_ec):
            ps = (pA, pB)[ec % 2]
            pguard(PN[id(ps)])
            for dc in range(n_dc):
                inc(t.matmul(ps[:, :], wbuf[:, dc * cols + ec * 128: dc * cols + ec * 128 + 128],
                             rhs_fn(dc), start=(dc == 0), stop=(dc == n_dc - 1)), "mm")
            sink(ec, ps)

    def load_xag_rhs(tt):
        blk, half = divmod(tt, 2)
        n = None
        for dd in range(0, NC16, 4):
            src = xag_out[blk].rearrange("(dc p) t -> p dc t", p=128)[:, dd:dd + 4,
                                                                     half * 512:half * 512 + 512]
            dstv = XL.rearrange("p (dc t) -> p dc t", dc=NC16)[:, dd:dd + 4, 0:512]
            n = incd(q.dma_start(out=dstv, in_=src), "ld")
        return n

    st_prev = {}

    def store_sink(dst_fn, bias_col=None):
        def sink(ec, ps):
            wt(sc, "mm")
            half = ec % 2
            stg = QSQ[:, half * 512:half * 512 + 512]
            if st_prev.get(("b", half)):
                wt(sc, "sb%d" % half, st_prev[("b", half)])
            if bias_col is not None:
                bi = sc.activation(out=stg, in_=ps[:, :], func=AF.Identity, bias=bias_col(ec))
            else:
                bi = sc.activation(out=stg, in_=ps[:, :], func=AF.Copy)
            n = inc(bi, "act")
            pdone(PN[id(ps)], "act")
            wt(q, "act", n)
            st_prev[("b", half)] = incd(q.dma_start(out=dst_fn(ec), in_=stg), "sb%d" % half)
        return sink

    def store_sink_f32(dst_fn):
        def sink(ec, ps):
            wt(v, "mm")
            half = ec % 2
            stg = OST[:, half * 512:half * 512 + 512]
            if st_prev.get(("f", half)):
                wt(v, "sf%d" % half, st_prev[("f", half)])
            n = inc(v.tensor_copy(out=stg, in_=ps[:, :]), "dve")
            pdone(PN[id(ps)], "dve")
            wt(q, "dve", n)
            st_prev[("f", half)] = incd(q.dma_start(out=dst_fn(ec), in_=stg), "sf%d" % half)
        return sink

    def ag_xln():
        wt(q, "dve")
        incd(q.dma_start(out=xag_in.rearrange("(dc p) t -> p dc t", p=128),
                         in_=XL.rearrange("p (dc t) -> p dc t", dc=NC16)), "st")
        wt_st(g)
        inc(g.collective_compute("AllGather", ALU.bypass, replica_groups=PAIRS,
                                 ins=[xag_in[:, :]], outs=[xag_out[:, :, :]]), "cc")
        wt(q, "cc")

    def proj_heads(w_list, dstL, wcache, bias_of=None):
        for wi, wd in enumerate(w_list):
            cached_w_load(wd, NC16, wcache)
            for tt in range(4):
                load_xag_rhs(tt)
                wt(t, "ld")
                wt(t, "gw")
                bias_col = None
                if bias_of is not None:
                    bias_col = lambda ec, _wi=wi: bias_of(_wi, ec)
                sink = store_sink(lambda ec, _wi=wi, _tt=tt: dstL[_wi, ec * 128:(ec + 1) * 128,
                                                                 _tt * 512:(_tt + 1) * 512],
                                  bias_col=bias_col)
                mm_cached(wcache, NC16, NHL, lambda dc: XL[:, dc * T: dc * T + 512], sink)
                barrier()

    def oproj_partial(w_my, mixL, bias_idx=None):
        wcache = BIG[:, 0:8192].bitcast(BF16)
        cached_w_load(w_my, NHL, wcache)
        wt_st(q)
        for tt in range(4):
            for dd in range(NHL):
                incd(q.dma_start(out=XL.rearrange("p (dc t) -> p dc t", dc=NC16)[:, dd, 0:512],
                                 in_=mixL[dd * 128:(dd + 1) * 128, tt * 512:(tt + 1) * 512]), "ld")
            wt(t, "ld")
            wt(t, "gw")
            blk, half = divmod(tt, 2)
            sink = store_sink_f32(lambda ec, _b=blk, _h=half: ops_in[_b, ec * 128:(ec + 1) * 128,
                                                                    _h * 512:(_h + 1) * 512])
            mm_cached(wcache, NHL, NC16, lambda dc: XL[:, dc * T: dc * T + 512], sink)
            barrier()
        wt_st(g)
        inc(g.collective_compute("ReduceScatter", ALU.add, replica_groups=PAIRS,
                                 ins=[ops_in[:, :, :]], outs=[ops_out[:, :]]), "cc")
        wt(q, "cc")
        for dc in range(NC16):
            wt(q, "dve")
            incd(q.dma_start(out=TMP[:, :], in_=ops_out[dc * 128:(dc + 1) * 128, 0:512]), "ld")
            incd(q.dma_start(out=TMP2[:, :], in_=ops_out[dc * 128:(dc + 1) * 128, 512:1024]), "ld")
            wt(v, "ld")
            for th, tb in ((0, TMP), (1, TMP2)):
                rsl = R[:, dc * T + th * 512: dc * T + th * 512 + 512]
                if bias_idx is not None:
                    bi = v.scalar_tensor_tensor(out=rsl, in0=tb[:, :],
                                                scalar=BIAS[:, bias_idx + dc:bias_idx + dc + 1],
                                                in1=rsl, op0=ALU.add, op1=ALU.add)
                else:
                    bi = v.tensor_add(out=rsl, in0=rsl, in1=tb[:, :])
                inc(bi, "dve")

    if "attn_a" in stages:
        emit_ln()
        ag_xln()
        barrier()
        proj_heads((wqTmy, wkTmy, wvTmy), aqkvL, BIG[:, 0:8192].bitcast(BF16))
        wt_st(q)
        qF = BIG[:, 0:1024].bitcast(BF16)
        qP = BIG[:, 1024:2048].bitcast(BF16)
        kP = BIG[:, 2048:3072].bitcast(BF16)
        kPT = BIG[:, 3072:4096].bitcast(BF16)
        vT = BIG[:, 4096:5120].bitcast(BF16)
        vF = BIG[:, 5120:6144].bitcast(BF16)
        maccb = BIG[:, 6144:7168].bitcast(BF16)
        lout = BIG[:, 7168:8192].bitcast(BF16)
        nrow = BIG[0:4, 8192:10240]
        crow = BIG[0:4, 10240:12288]
        oout = BIG[:, 12288:13312].bitcast(BF16)
        tsc1 = BIG[:, 13312:14336].bitcast(BF16)
        rdsb = BIG[:, 14336:14464]
        qpm = BIG[:, 14464:14720].bitcast(BF16)
        for gl in range(NHL):
            wt(q, "dve")
            wt_st(q)
            for wi, dstt in ((0, qF), (1, kP), (2, vF)):
                incd(q.dma_start(out=dstt[:, :], in_=aqkvL[wi, gl * 128:(gl + 1) * 128, :]), "ld")
            wt(g, "mm")
            incd(g.dma_start(out=MEMW.rearrange("p (m e) -> p m e", m=M),
                             in_=memsI[:, gl, :, :].transpose([1, 0, 2])), "gw")
            wt(sc, "ld")
            wt(v, "ld")
            for (src, dst) in ((qF, qP), (kP, kP)):
                wt(sc, "dve")
                inc(sc.activation(out=tsc1[:, :], in_=src[:, :], func=AF.Exp), "act")
                wt(v, "act")
                vfl(v.tensor_scalar_min(out=tsc1[:, :], in0=tsc1[:, :], scalar1=1.0))
                inc(v.scalar_tensor_tensor(out=dst[:, :], in0=src[:, :], scalar=0.0,
                                           in1=tsc1[:, :], op0=ALU.max, op1=ALU.add), "dve")
            wt(t, "dve")
            for (srcT, dstT) in ((kP, kPT), (vF, vT)):
                for ch in range(16):
                    ps = (pC, pD)[ch % 2]
                    psb = ps[:, 0:64].bitcast(BF16)
                    pguard(PN[id(ps)])
                    inc(t.transpose(psb, srcT[:, ch * 128:(ch + 1) * 128], IDT[:, :]), "mm")
                    wt(v, "mm")
                    inc(v.tensor_copy(out=dstT[:, ch * 128:(ch + 1) * 128], in_=psb), "dve")
                    pdone(PN[id(ps)], "dve")
            wt(t, "gw")
            for st4 in range(4):
                pguard("pS1")
                wt(t, "dve")
                inc(t.matmul(pS1[0:4, :], MNT[:, gl * M:(gl + 1) * M], qP[:, st4 * 512:(st4 + 1) * 512],
                             start=True, stop=True), "mm")
                wt(v, "mm")
                inc(v.tensor_copy(out=nrow[:, st4 * 512:(st4 + 1) * 512], in_=pS1[0:4, :]), "dve")
                pdone("pS1", "dve")
            vfl(v.tensor_reduce(out=GCOL[:, 4:5], in_=qF[:, :], axis=AX.X, op=ALU.add))
            inc(v.tensor_copy(out=QSQ[:, 0:1], in_=GCOL[:, 4:5]), "dve")
            wt(t, "dve")
            pguard("pS2")
            inc(t.matmul(pS2[0:1, 0:4], QSQ[:, 0:1], MNT[:, gl * M:(gl + 1) * M],
                         start=True, stop=True), "mm")
            wt(v, "mm")
            relm = SROW[0:1, 40:44]
            vfl(v.tensor_scalar_mul(out=relm, in0=pS2[0:1, 0:4], scalar1=SCALE / S))
            pdone("pS2", "dve")
            vfl(v.tensor_scalar(out=SROW[0:1, 44:48], in0=SROW[0:1, 60:64], scalar1=1e30,
                                scalar2=-1e30, op0=ALU.mult, op1=ALU.add))
            vfl(v.tensor_add(out=relm, in0=relm, in1=SROW[0:1, 44:48]))
            vfl(v.tensor_reduce(out=SROW[0:1, 48:49], in_=relm, axis=AX.X, op=ALU.max))
            vfl(v.scalar_tensor_tensor(out=relm, in0=relm, scalar=0.0,
                                       in1=SROW[0:1, 48:49].to_broadcast([1, 4]),
                                       op0=ALU.add, op1=ALU.subtract))
            wt(sc, "dve")
            inc(sc.activation(out=relm, in_=relm, func=AF.Exp), "act")
            wt(v, "act")
            vfl(v.tensor_reduce(out=SROW[0:1, 48:49], in_=relm, axis=AX.X, op=ALU.add))
            vfl(v.reciprocal(out=SROW[0:1, 49:50], in_=SROW[0:1, 48:49]))
            vfl(v.scalar_tensor_tensor(out=relm, in0=relm, scalar=0.0,
                                       in1=SROW[0:1, 49:50].to_broadcast([1, 4]),
                                       op0=ALU.add, op1=ALU.mult))
            vfl(v.tensor_mul(out=relm, in0=relm, in1=SROW[0:1, 60:64]))
            vfl(v.scalar_tensor_tensor(out=relm, in0=relm, scalar=0.0,
                                       in1=GROW[0:1, 16 + gl:17 + gl].to_broadcast([1, 4]),
                                       op0=ALU.add, op1=ALU.mult))
            wt(t, "dve")
            pguard("pS2")
            inc(t.matmul(pS2[0:4, 0:1], relm, ONESF[0:1, 0:1], start=True, stop=True), "mm")
            wt(v, "mm")
            vfl(v.tensor_copy(out=GCOL[0:4, 6:7], in_=pS2[0:4, 0:1]))
            pdone("pS2", "dve")
            vfl(v.tensor_scalar_max(out=nrow[:, :], in0=nrow[:, :], scalar1=EPS_MEM))
            vfl(v.reciprocal(out=crow[:, :], in_=nrow[:, :]))
            inc(v.tensor_scalar_mul(out=crow[:, :], in0=crow[:, :], scalar1=GCOL[0:4, 6:7]), "dve")
            for st4 in range(4):
                pguard("pB")
                for m4 in range(M):
                    pguard("pE")
                    wt(t, "dve")
                    inc(t.matmul(pE[:, :], SEL4[0:4, m4 * 128:(m4 + 1) * 128],
                                 crow[0:4, st4 * 512:(st4 + 1) * 512],
                                 start=True, stop=True), "mm")
                    wt(v, "mm")
                    inc(v.tensor_mul(out=qpm[:, :], in0=qP[:, st4 * 512:(st4 + 1) * 512], in1=pE[:, :]), "dve")
                    pdone("pE", "dve")
                    wt(t, "dve")
                    inc(t.matmul(pB[:, :], MEMW[:, m4 * HD:(m4 + 1) * HD], qpm[:, :],
                                 start=(m4 == 0), stop=(m4 == M - 1)), "mm")
                wt(v, "mm")
                inc(v.tensor_copy(out=maccb[:, st4 * 512:(st4 + 1) * 512], in_=pB[:, :]), "dve")
                pdone("pB", "dve")
            inc(v.memset(SST[:, :], 0.0), "dve")
            inc(v.memset(SSTB[:, :], 0.0), "dve")
            inc(v.memset(ZST[:, 0:1], 0.0), "dve")
            vfl(v.memset(ZSTB[:, 0:1], 0.0))
            for ch in range(16):
                o = ch * 128
                wt(t, "dve")
                pguard("pC")
                pguard("pD")
                pguard("pA")
                pguard("pS1")
                inc(t.matmul(pC[:, 0:128], kP[:, o:o + 128], qP[:, o:o + 128], start=True, stop=True), "mm")
                inc(t.matmul(pD[:, 0:128], kPT[:, o:o + 128], vT[:, o:o + 128], start=True, stop=True), "mm")
                inc(t.matmul(pA[:, 0:128], SSTB[:, :], qP[:, o:o + 128], start=True, stop=False), "mm")
                inc(t.matmul(pS1[0:1, 0:128], ZSTB[:, 0:1], qP[:, o:o + 128], start=True, stop=False), "mm")
                wt(v, "mm")
                mskd = tsc1[:, 0:128]
                inc(v.tensor_mul(out=mskd, in0=pC[:, 0:128], in1=TRIU[:, :]), "dve")
                pdone("pC", "dve")
                wt(t, "dve")
                inc(t.matmul(pA[:, 0:128], vT[:, o:o + 128], mskd, start=False, stop=True), "mm")
                inc(t.matmul(pS1[0:1, 0:128], ONESB[:, 0:1], mskd, start=False, stop=True), "mm")
                wt(v, "mm")
                den = P0[0:1, 0:128]
                vfl(v.tensor_single_scalar(out=den, in_=pS1[0:1, 0:128], scalar=EPS_MEM, op=ALU.max))
                inc(v.reciprocal(out=den, in_=den), "dve")
                pdone("pS1", "dve")
                wt(t, "dve")
                pguard("pF")
                inc(t.matmul(pF[:, 0:128], ONESRF[0:1, :], den, start=True, stop=True), "mm")
                wt(v, "mm")
                vfl(v.tensor_copy(out=rdsb[:, :], in_=pF[:, 0:128]))
                pdone("pF", "dve")
                inc(v.tensor_mul(out=lout[:, o:o + 128], in0=pA[:, 0:128], in1=rdsb[:, :]), "dve")
                pdone("pA", "dve")
                vfl(v.tensor_add(out=SST[:, :], in0=SST[:, :], in1=pD[:, 0:128]))
                pdone("pD", "dve")
                inc(v.tensor_copy(out=SSTB[:, :], in_=SST[:, :]), "dve")
                vfl(v.tensor_reduce(out=ZST[:, 1:2], in_=kP[:, o:o + 128], axis=AX.X, op=ALU.add))
                vfl(v.tensor_add(out=ZST[:, 0:1], in0=ZST[:, 0:1], in1=ZST[:, 1:2]))
                inc(v.tensor_copy(out=ZSTB[:, 0:1], in_=ZST[:, 0:1]), "dve")
            wt(t, "dve")
            pguard("pE")
            inc(t.matmul(pE[:, 0:1], ONESRF[0:1, :], GROW[0:1, 8 + gl:9 + gl], start=True, stop=True), "mm")
            wt(v, "mm")
            vfl(v.tensor_copy(out=GCOL[:, 7:8], in_=pE[:, 0:1]))
            pdone("pE", "dve")
            wt(v, "st")
            inc(v.scalar_tensor_tensor(out=oout[:, :], in0=lout[:, :], scalar=GCOL[:, 7:8],
                                       in1=maccb[:, :], op0=ALU.mult, op1=ALU.add), "dve")
            wt(q, "dve")
            incd(q.dma_start(out=amixL[gl * 128:(gl + 1) * 128, :], in_=oout[:, :]), "st")
            if PARANOID & 8:
                barrier()
        barrier()
        oproj_partial(woTmy, amixL)
        barrier()

    def emit_ffn():
        if PARANOID & 1:
            barrier()
        emit_ln()
        if PARANOID & 2:
            barrier()
        h = BIG.bitcast(BF16)
        WU = (WS[:, 0:1024].bitcast(BF16), WS[:, 1024:2048].bitcast(BF16))
        WG = (WS[:, 2048:3072].bitcast(BF16), WS[:, 3072:4096].bitcast(BF16))
        for th in range(2):
            wt(t, "dve")
            mm_buf = {}
            for ec in range(64):
                b = ec % 2
                if ec >= 2:
                    wt(q, "mm", mm_buf[ec - 2])
                wt(q, "cc", cc_tern)
                ws_ = "w%d" % b
                incd(q.dma_start(out=WU[b].rearrange("p (dc e) -> p dc e", dc=NC16),
                                 in_=wu_q[ec]), ws_)
                incd(q.dma_start(out=WG[b].rearrange("p (dc e) -> p dc e", dc=NC16),
                                 in_=wg_q[ec]), ws_)
                wt(t, ws_)
                psu = (pA, pB)[b]
                psg = (pC, pD)[b]
                pguard(PN[id(psu)])
                pguard(PN[id(psg)])
                for dc in range(NC16):
                    inc(t.matmul(psu[:, :], WU[b][:, dc * 128:(dc + 1) * 128],
                                 XL[:, dc * T + th * 512: dc * T + th * 512 + 512],
                                 start=(dc == 0), stop=(dc == NC16 - 1)), "mm")
                for dc in range(NC16):
                    inc(t.matmul(psg[:, :], WG[b][:, dc * 128:(dc + 1) * 128],
                                 XL[:, dc * T + th * 512: dc * T + th * 512 + 512],
                                 start=(dc == 0), stop=(dc == NC16 - 1)), "mm")
                mm_buf[ec] = cnt["mm"]
                wt(sc, "mm")
                hs = QSQ[:, (b) * 512:(b) * 512 + 512]
                inc(sc.activation(out=hs, in_=psu[:, :], func=AF.Silu, scale=SCOL[:, 0:1]), "act")
                pdone(PN[id(psu)], "act")
                wt(v, "act")
                inc(v.scalar_tensor_tensor(out=h[:, ec * 512:(ec + 1) * 512], in0=psg[:, :],
                                           scalar=SCOL[:, 1:2], in1=hs, op0=ALU.mult, op1=ALU.mult), "dve")
                pdone(PN[id(psg)], "dve")
            wt(q, "mm")
            wt(t, "dve")
            mm_d = {}
            for dc in range(NC16):
                psd = (pE, pF)[dc % 2]
                pguard(PN[id(psd)])
                for qa in range(4):
                    b = qa % 2
                    key = (dc, qa)
                    prev = (dc, qa - 2) if qa >= 2 else ((dc - 1, qa + 2) if dc >= 1 else None)
                    if prev is not None and prev in mm_d:
                        wt(q, "mm", mm_d[prev])
                    incd(q.dma_start(out=WU[b].rearrange("p (r a d) -> p r a d", r=2, a=8),
                                     in_=wd_q.rearrange("(r c) p a d -> r c p a d", r=8)[2 * qa:2 * qa + 2, dc]
                                     .transpose([1, 0, 2, 3])),
                         "w%d" % b)
                    wt(t, "w%d" % b)
                    for ea in range(16):
                        inc(t.matmul(psd[:, :], WU[b][:, ea * 128:(ea + 1) * 128],
                                     h[:, (qa * 16 + ea) * 512:(qa * 16 + ea) * 512 + 512],
                                     start=(qa == 0 and ea == 0), stop=(qa == 3 and ea == 15)), "mm")
                    mm_d[key] = cnt["mm"]
                wt(v, "mm")
                rsl = R[:, dc * T + th * 512: dc * T + th * 512 + 512]
                inc(v.scalar_tensor_tensor(out=rsl, in0=psd[:, :], scalar=SCOL[:, 2:3],
                                           in1=rsl, op0=ALU.mult, op1=ALU.add), "dve")
                pdone(PN[id(psd)], "dve")
            barrier()

    if "q_dbg" in stages:
        wt(q, "cc", cc_tern)
        wt(v, "cc", cc_tern)
        wt(g, "cc", cc_tern)
        inc(v.tensor_copy(out=R[0:1, 0:64], in_=SROW[0:1, 0:64]), "dve")
        inc(v.tensor_copy(out=R[0:128, 64:76], in_=SCOL[:, 0:12]), "dve")
        barrier()

    if "ln_dbg" in stages:
        emit_ln()
        wt(v, "mm")
        for dc in range(NC16):
            inc(v.tensor_copy(out=R[:, dc * T:(dc + 1) * T], in_=XL[:, dc * T:(dc + 1) * T]), "dve")

    if "ffn1" in stages:
        emit_ffn()

    if "attn_b" in stages:
        emit_ln()
        ag_xln()
        barrier()
        QF = BIG[:, 0:8192].bitcast(BF16)
        mm_w = {}
        for ec in range(NC16):
            b = ec % 2
            wb = WS[:, b * 1024:(b + 1) * 1024].bitcast(BF16)
            if ec >= 2:
                wt(g, "mm", mm_w[ec - 2])
            incd(g.dma_start(out=wb.rearrange("p (dc e) -> p dc e", dc=NC16),
                             in_=bWq.rearrange("(dc p) e -> p dc e", p=128)[:, :, ec * 128:(ec + 1) * 128]),
                 "g%d" % b)
            wt(t, "g%d" % b)
            for th in range(2):
                ps = (pA, pB)[th]
                pguard(PN[id(ps)])
                for dc in range(NC16):
                    inc(t.matmul(ps[:, :], wb[:, dc * 128:(dc + 1) * 128],
                                 XL[:, dc * T + th * 512: dc * T + th * 512 + 512],
                                 start=(dc == 0), stop=(dc == NC16 - 1)), "mm")
                mm_w[ec] = cnt["mm"]
                wt(sc, "mm")
                inc(sc.activation(out=QF[:, ec * T + th * 512: ec * T + th * 512 + 512],
                                  in_=ps[:, :], func=AF.Identity, bias=BIAS[:, ec:ec + 1]), "act")
                pdone(PN[id(ps)], "act")
        wt(t, "act")
        barrier()
        proj_heads((bWkmy, bWvmy), bkvL, BIG[:, 8192:16384].bitcast(BF16),
                   bias_of=lambda wi, ec: BIAS[:, 16 + wi * 8 + ec: 16 + wi * 8 + ec + 1])
        wt_st(q)
        kF = BIG[:, 12288:13312].bitcast(BF16)
        vF = BIG[:, 13312:14336].bitcast(BF16)
        vT = BIG[:, 14336:15360].bitcast(BF16)
        outst = BIG[:, 15360:15616].bitcast(BF16)
        denb = BIG[:, 15616:16128]
        PT = BIG[:, 8192:12288].bitcast(BF16)
        QF3 = QF.rearrange("p (m tk) -> p tk m", m=16)
        for gl in range(NHL):
            wt(q, "dve")
            wt_st(q)
            wt(q, "act")
            incd(q.dma_start(out=kF[:, :], in_=bkvL[0, gl * 128:(gl + 1) * 128, :]), "ld")
            incd(q.dma_start(out=vF[:, :], in_=bkvL[1, gl * 128:(gl + 1) * 128, :]), "ld")
            wt(t, "ld")
            wt(v, "ld")
            for ch in range(16):
                ps = (pC, pD)[ch % 2]
                psb = ps[:, 0:64].bitcast(BF16)
                pguard(PN[id(ps)])
                inc(t.transpose(psb, vF[:, ch * 128:(ch + 1) * 128], IDT[:, :]), "mm")
                wt(v, "mm")
                inc(v.tensor_copy(out=vT[:, ch * 128:(ch + 1) * 128], in_=psb), "dve")
                pdone(PN[id(ps)], "dve")
            vfl(v.memset(SROW[0:1, 50:52], 0.0))
            for hf in range(2):
                inc(v.tensor_mul(out=QSQ.rearrange("p (tk m) -> p tk m", m=8),
                                 in0=QF3[:, gl * 128:(gl + 1) * 128, hf * 8:(hf + 1) * 8],
                                 in1=QF3[:, gl * 128:(gl + 1) * 128, hf * 8:(hf + 1) * 8]), "dve")
                pguard("pS1")
                for m4 in range(2):
                    wt(t, "dve")
                    inc(t.matmul(pS1[0:1, :], ONESB[:, 0:1], QSQ[:, m4 * 512:(m4 + 1) * 512],
                                 start=True, stop=True), "mm")
                    wt(v, "mm")
                    vfl(v.tensor_reduce(out=SROW[0:1, 52:53], in_=pS1[0:1, :], axis=AX.X, op=ALU.max))
                    vfl(v.tensor_tensor(out=SROW[0:1, 50:51], in0=SROW[0:1, 50:51],
                                        in1=SROW[0:1, 52:53], op=ALU.max))
                pdone("pS1", "dve")
            for hf in range(2):
                inc(v.tensor_mul(out=QSQ[:, :], in0=kF[:, hf * 1024:(hf + 1) * 1024],
                                 in1=kF[:, hf * 1024:(hf + 1) * 1024]), "dve")
                pguard("pS1")
                for m4 in range(2):
                    wt(t, "dve")
                    inc(t.matmul(pS1[0:1, :], ONESB[:, 0:1], QSQ[:, m4 * 512:(m4 + 1) * 512],
                                 start=True, stop=True), "mm")
                    wt(v, "mm")
                    vfl(v.tensor_reduce(out=SROW[0:1, 53:54], in_=pS1[0:1, :], axis=AX.X, op=ALU.max))
                    vfl(v.tensor_tensor(out=SROW[0:1, 51:52], in0=SROW[0:1, 51:52],
                                        in1=SROW[0:1, 53:54], op=ALU.max))
                pdone("pS1", "dve")
            vfl(v.tensor_add(out=SROW[0:1, 54:55], in0=SROW[0:1, 50:51], in1=SROW[0:1, 51:52]))
            inc(v.tensor_scalar_mul(out=SROW[0:1, 54:55], in0=SROW[0:1, 54:55],
                                    scalar1=-0.51 * SCALE), "dve")
            wt(t, "dve")
            pguard("pS2")
            inc(t.matmul(pS2[:, 0:1], ONESRF[0:1, :], SROW[0:1, 54:55], start=True, stop=True), "mm")
            wt(v, "mm")
            inc(v.tensor_copy(out=GCOL[:, 5:6], in_=pS2[:, 0:1]), "dve")
            pdone("pS2", "dve")
            wt(sc, "dve")
            for jt in range(4):
                na = 4 * jt + 4
                rhs = QF3[:, gl * 128 + 32 * jt: gl * 128 + 32 * jt + 32, :]
                for a in range(na):
                    ps = (pC, pD)[a % 2]
                    pguard(PN[id(ps)])
                    inc(t.matmul(ps[:, :], kF[:, a * 128:(a + 1) * 128], rhs, start=True, stop=True), "mm")
                    wt(sc, "mm")
                    inc(sc.activation(out=PT[:, a * 512:(a + 1) * 512], in_=ps[:, :], func=AF.Exp,
                                      bias=GCOL[:, 5:6], scale=SCALE), "act")
                    pdone(PN[id(ps)], "act")
                wt(v, "act")
                for r in range(4):
                    a = 4 * jt + r
                    inc(v.tensor_mul(out=PT[:, a * 512:(a + 1) * 512], in0=PT[:, a * 512:(a + 1) * 512],
                                     in1=BMASK[:, r * 512:(r + 1) * 512]), "dve")
                wt(t, "dve")
                pguard("pS1")
                for a in range(na):
                    inc(t.matmul(pS1[0:1, :], ONESB[:, 0:1], PT[:, a * 512:(a + 1) * 512],
                                 start=(a == 0), stop=(a == na - 1)), "mm")
                wt(v, "mm")
                vfl(v.tensor_single_scalar(out=P0[0:1, 0:512], in_=pS1[0:1, :], scalar=1e-30,
                                           op=ALU.max))
                inc(v.reciprocal(out=P0[0:1, 0:512], in_=P0[0:1, 0:512]), "dve")
                pdone("pS1", "dve")
                wt(t, "dve")
                pguard("pS2")
                inc(t.matmul(pS2[:, :], ONESRF[0:1, :], P0[0:1, 0:512], start=True, stop=True), "mm")
                wt(v, "mm")
                vfl(v.tensor_copy(out=denb[:, :], in_=pS2[:, :]))
                pdone("pS2", "dve")
                pguard("pA")
                for a in range(na):
                    inc(t.matmul(pA[:, :], vT[:, a * 128:(a + 1) * 128], PT[:, a * 512:(a + 1) * 512],
                                 start=(a == 0), stop=(a == na - 1)), "mm")
                wt(v, "mm")
                if st_prev.get("bo"):
                    wt(v, "st", st_prev["bo"])
                inc(v.tensor_mul(out=outst[:, :], in0=pA[:, :], in1=denb[:, :]), "dve")
                pdone("pA", "dve")
                wt(q, "dve")
                st_prev["bo"] = incd(q.dma_start(out=bmixL[gl * 128:(gl + 1) * 128,
                                                           jt * 512:(jt + 1) * 512],
                                                 in_=outst[:, :]), "st")
            if PARANOID & 4:
                barrier()
        barrier()
        oproj_partial(bWomy, bmixL, bias_idx=32)
        barrier()

    if "ffn2" in stages:
        emit_ffn()

    wt(q, "dve")
    incd(q.dma_start(out=outT.rearrange("(c p) t -> p c t", p=128),
                     in_=R.rearrange("p (c t) -> p c t", c=NC16)), "st")
    wt_st(q)

    es.close()
    return nc


def _mk_consts():
    ident = np.eye(128, dtype=np.float32)
    triu = np.triu(np.ones((128, 128), dtype=np.float32))
    bmask = np.zeros((128, 4, 512), dtype=np.float32)
    sl = np.arange(128)[:, None]
    jl = np.arange(512)[None, :]
    for r in range(4):
        bmask[:, r, :] = (sl <= (jl - 128 * r)).astype(np.float32)
    ones_c = np.ones((128, 2), dtype=np.float32)
    ones_r = np.ones((1, 128), dtype=np.float32)
    sel4 = np.zeros((4, 4, 128), dtype=np.float32)
    for m in range(4):
        sel4[m, m, :] = 1.0
    return ident, triu, bmask.reshape(128, 2048), ones_c, ones_r, sel4.reshape(4, 512)


_NC_CACHE = {}


def _get_nc(stages):
    key = tuple(stages)
    if key not in _NC_CACHE:
        _NC_CACHE[key] = build(stages)
    return _NC_CACHE[key]


def kernel(L0_attn_a_hidden_states, L0_attn_a_memories, L0_attn_a_memory_norms,
           L0_attn_a_w_q, L0_attn_a_w_k, L0_attn_a_w_v, L0_attn_a_w_o, L0_attn_a_gate,
           L0_attn_b_Wq, L0_attn_b_bq, L0_attn_b_Wk, L0_attn_b_bk, L0_attn_b_Wv,
           L0_attn_b_bv, L0_attn_b_Wo, L0_attn_b_bo, L0_ffn_W_up, L0_ffn_W_gate,
           L0_ffn_W_down, _stages=ALL_STAGES):
    x = np.ascontiguousarray(np.asarray(L0_attn_a_hidden_states, dtype=np.float32))
    B = x.shape[0]
    mem = np.asarray(L0_attn_a_memories, dtype=np.float32)
    mnorm = np.asarray(L0_attn_a_memory_norms, dtype=np.float32)
    awq = np.asarray(L0_attn_a_w_q, dtype=np.float32)
    awk = np.asarray(L0_attn_a_w_k, dtype=np.float32)
    awv = np.asarray(L0_attn_a_w_v, dtype=np.float32)
    awo = np.asarray(L0_attn_a_w_o, dtype=np.float32)
    gate = np.asarray(L0_attn_a_gate, dtype=np.float32)
    bwq = np.ascontiguousarray(np.asarray(L0_attn_b_Wq, dtype=np.float32))
    bwk = np.asarray(L0_attn_b_Wk, dtype=np.float32)
    bwv = np.asarray(L0_attn_b_Wv, dtype=np.float32)
    bwo = np.asarray(L0_attn_b_Wo, dtype=np.float32)
    bq_ = np.asarray(L0_attn_b_bq, dtype=np.float32)
    bk_ = np.asarray(L0_attn_b_bk, dtype=np.float32)
    bv_ = np.asarray(L0_attn_b_bv, dtype=np.float32)
    bo_ = np.asarray(L0_attn_b_bo, dtype=np.float32)
    wu = np.asarray(L0_ffn_W_up, dtype=np.float32)
    wg = np.asarray(L0_ffn_W_gate, dtype=np.float32)
    wd = np.asarray(L0_ffn_W_down, dtype=np.float32)

    awqT = np.ascontiguousarray(awq.T)
    awkT = np.ascontiguousarray(awk.T)
    awvT = np.ascontiguousarray(awv.T)
    awoT = np.ascontiguousarray(awo.T)
    bwoT = np.ascontiguousarray(bwo)
    wuT = np.ascontiguousarray(wu.T)
    wgT = np.ascontiguousarray(wg.T)
    wdT = np.ascontiguousarray(wd.T)
    ident, triu, bmask, ones_c, ones_r, sel4 = _mk_consts()
    mnF = np.ascontiguousarray(mnorm.transpose(2, 0, 1).reshape(128, 64))

    in_maps = []
    for c in range(8):
        b, hg = divmod(c, 2)
        heads = slice(8 * hg, 8 * hg + 8)
        chs = slice(1024 * hg, 1024 * hg + 1024)
        im = {
            "xT": np.ascontiguousarray(x[b, 1024 * hg:1024 * hg + 1024, :].T),
            "wqTmy": np.ascontiguousarray(awqT[:, chs]),
            "wkTmy": np.ascontiguousarray(awkT[:, chs]),
            "wvTmy": np.ascontiguousarray(awvT[:, chs]),
            "woTmy": np.ascontiguousarray(awoT[chs, :]),
            "memsI": np.ascontiguousarray(mem[:, heads, :, :]),
            "mnT": np.ascontiguousarray(mnorm[:, heads, :].transpose(1, 2, 0)),
            "mnF": mnF,
            "gateL": np.ascontiguousarray(gate[heads].reshape(1, 8)),
            "bWq": bwq,
            "bWkmy": np.ascontiguousarray(bwk[:, chs]),
            "bWvmy": np.ascontiguousarray(bwv[:, chs]),
            "bWomy": np.ascontiguousarray(bwoT[chs, :]),
            "bbq": np.ascontiguousarray(bq_.reshape(16, 128).T),
            "bbkv": np.ascontiguousarray(np.concatenate(
                [bk_[1024 * hg:1024 * hg + 1024].reshape(8, 128).T,
                 bv_[1024 * hg:1024 * hg + 1024].reshape(8, 128).T], axis=1)),
            "bbo": np.ascontiguousarray(bo_.reshape(16, 128).T),
            "wuS": np.ascontiguousarray(wuT[:, 1024 * c:1024 * (c + 1)]),
            "wgS": np.ascontiguousarray(wgT[:, 1024 * c:1024 * (c + 1)]),
            "wdS": np.ascontiguousarray(wdT[1024 * c:1024 * (c + 1), :]),
            "identC": ident,
            "sel4C": sel4,
            "triuC": triu,
            "bmaskC": bmask,
            "onesC": ones_c,
            "onesR": ones_r,
        }
        in_maps.append(im)

    nc = _get_nc(_stages)
    res = run_bass_kernel_spmd(nc, in_maps, core_ids=list(range(8)))
    out = np.empty((B, 2048, 2048), dtype=np.float32)
    for c in range(8):
        b, hg = divmod(c, 2)
        out[b, 1024 * hg:1024 * hg + 1024, :] = res.results[c]["outT"].T
    return out


if __name__ == "__main__":
    import reference as _r
    inp = _r.setup_inputs(0)
    outs = kernel(**{k: np.asarray(v) for k, v in inp.items()})
    ref = np.asarray(_r.reference(**inp))
    err = np.abs(outs - ref)
    denom = np.sqrt(np.mean(ref ** 2))
    print("rel_l2:", np.sqrt(np.mean((outs - ref) ** 2)) / denom, "max_abs:", err.max())
